# version 10
# baseline (speedup 1.0000x reference)
from contextlib import ExitStack

import numpy as np
import concourse.bass as bass
import concourse.mybir as mybir
from concourse.bass_utils import run_bass_kernel_spmd

F32 = mybir.dt.float32
BF16 = mybir.dt.bfloat16
AF = mybir.ActivationFunctionType
ALU = mybir.AluOpType
AX = mybir.AxisListType

N_CORES = 8
SEQ = 2048
D = 1024
SPC = 4
NT = SEQ // 128
NC_ = D // 128
LN_EPS = 1e-5

ENGINES = ("sync", "vector", "tensor", "scalar", "gpsimd")


class Sched:

    def __init__(self, sems, nc=None):
        self.ops = {e: [] for e in ENGINES}
        self.sems = sems
        self.nc = nc
        self.cnt = {name: 0 for name in sems}
        self.frames = []
        self.last_wait = {}
        self.ctrs = {}
        self.loop_vars = {e: [] for e in ENGINES}

    def raw(self, e, fn):
        self.ops[e].append(fn)

    def inc(self, sem, n):
        self.cnt[sem] += n
        return self.cnt[sem]

    def call(self, e, meth, *args, inc=None, **kw):
        sems = self.sems
        if inc is not None:
            self.inc(*inc)

        def fn(eng):
            ins = getattr(eng, meth)(*args, **kw)
            if inc is not None:
                ins.then_inc(sems[inc[0]], inc[1])
        self.raw(e, fn)

    def _ctr(self, e, sem):
        key = (e, sem)
        if key not in self.ctrs:
            ctr = Ctr.__new__(Ctr)
            ctr.eng = None
            ctr.reg = None
            ctr.name = f"cw_{e}_{sem}"
            self.ctrs[key] = ctr
            init_val = self.frames[0]["entry_cnt"][sem]
            self.last_wait[key] = init_val
            nc = self.nc

            def init_fn(eng, ctr=ctr, iv=init_val):
                regs = nc.alloc_registers(ctr.name, engines=[eng.engine])
                ctr.reg = list(regs)[0]
                ctr.eng = eng
                nc.reg_mov(ctr.reg, iv)
            self.frames[0]["init_ops"][e].append(init_fn)
            for fr in self.frames:
                fr["entry_last"].setdefault(key, self.last_wait[key])
        return self.ctrs[key]

    def wait(self, e, sem, val=None):
        v = self.cnt[sem] if val is None else val
        if v <= 0:
            return
        h = self.sems[sem]
        if not self.frames:
            self.raw(e, lambda eng: eng.wait_ge(h, v))
            return
        ctr = self._ctr(e, sem)
        key = (e, sem)
        delta = v - self.last_wait[key]
        if delta < 0:
            return
        self.last_wait[key] = v
        self.frames[-1]["used"].add(key)

        def fn(eng, ctr=ctr, d=delta):
            if d:
                eng.scalar_reg_alu(ALU.add, ctr.reg, d)
            eng.wait_ge(h, ctr.reg)
        self.raw(e, fn)

    def begin_loop(self, trip):
        fr = {"trip": trip, "entry_cnt": dict(self.cnt),
              "entry_last": {k: self.last_wait[k] for k in self.last_wait},
              "used": set(), "init_ops": {e: [] for e in ENGINES},
              "id": len(self.frames)}
        self.frames.append(fr)
        marker = ("LOOP_START", fr)
        for e in ENGINES:
            self.ops[e].append(marker)
        return fr

    def end_loop(self):
        fr = self.frames.pop()
        trip = fr["trip"]
        totals = {s: self.cnt[s] - fr["entry_cnt"][s] for s in self.cnt}
        for key in fr["used"]:
            e, sem = key
            ctr = self.ctrs[key]
            adv = self.last_wait[key] - fr["entry_last"].get(key, self.last_wait[key])
            drift = totals[sem] - adv

            def fn(eng, ctr=ctr, d=drift):
                if d:
                    eng.scalar_reg_alu(ALU.add, ctr.reg, d)
            self.ops[e].append(fn)
            self.last_wait[key] = fr["entry_last"].get(key, self.last_wait[key]) \
                + trip * totals[sem]
        for s in self.cnt:
            self.cnt[s] = fr["entry_cnt"][s] + trip * totals[s]
        marker = ("LOOP_END", fr)
        for e in ENGINES:
            self.ops[e].append(marker)
        if self.frames:
            self.frames[-1]["used"] |= fr["used"]
            for key in fr["used"]:
                self.frames[-1]["entry_last"].setdefault(key, None)

    def dma(self, e, out, in_, sem):
        self.call(e, "dma_start", inc=(sem, 16), out=out, in_=in_)

    def chained(self, e, sem, meth, *args, **kw):
        self.call(e, meth, *args, inc=(sem, 1), **kw)
        self.wait(e, sem)

    def mm_group(self, mms, inc):
        sems = self.sems
        self.inc(*inc)
        n = len(mms)

        def fn(eng):
            ins = None
            for j, (o, l, r) in enumerate(mms):
                ins = eng.matmul(out=o, lhsT=l, rhs=r,
                                 start=(j == 0), stop=(j == n - 1))
            ins.then_inc(sems[inc[0]], inc[1])
        self.raw("tensor", fn)

    def mm_multi(self, groups, inc):
        sems = self.sems
        self.inc(*inc)

        def fn(eng):
            ins = None
            for grp in groups:
                n = len(grp)
                for j, (o, l, r) in enumerate(grp):
                    ins = eng.matmul(out=o, lhsT=l, rhs=r,
                                     start=(j == 0), stop=(j == n - 1))
            ins.then_inc(sems[inc[0]], inc[1])
        self.raw("tensor", fn)

    def transposes(self, trs, identity, inc):
        sems = self.sems
        self.inc(*inc)

        def fn(eng):
            ins = None
            for (o, i_) in trs:
                ins = eng.transpose(out=o, in_=i_, identity=identity)
            ins.then_inc(sems[inc[0]], inc[1])
        self.raw("tensor", fn)


def _run_ops(S, e, eng, ops, idx):
    from contextlib import ExitStack
    while idx < len(ops):
        item = ops[idx]
        if isinstance(item, tuple) and item[0] == "LOOP_START":
            fr = item[1]
            for f in fr["init_ops"][e]:
                f(eng)
            with eng.Fori(0, fr["trip"]) as v:
                S.loop_vars[e].append(v)
                idx = _run_ops(S, e, eng, ops, idx + 1)
                S.loop_vars[e].pop()
        elif isinstance(item, tuple) and item[0] == "LOOP_END":
            return idx
        else:
            item(eng)
        idx += 1
    return idx


def emit_block(nc, S):
    with nc.Block() as block:
        @block.sync
        def _(eng):
            _run_ops(S, "sync", eng, S.ops["sync"], 0)

        @block.vector
        def _(eng):
            _run_ops(S, "vector", eng, S.ops["vector"], 0)

        @block.tensor
        def _(eng):
            _run_ops(S, "tensor", eng, S.ops["tensor"], 0)

        @block.scalar
        def _(eng):
            _run_ops(S, "scalar", eng, S.ops["scalar"], 0)

        @block.gpsimd
        def _(eng):
            _run_ops(S, "gpsimd", eng, S.ops["gpsimd"], 0)


class Bufs:
    pass


def alloc_bufs(nc, ctx):
    b = Bufs()
    b.nc = nc
    ent = ctx.enter_context
    b.XRES = ent(nc.sbuf_tensor("XRES", [128, NT * D], F32))
    b.XFM = ent(nc.sbuf_tensor("XFM", [128, NC_ * SEQ], BF16))
    b.BUF_A = ent(nc.sbuf_tensor("BUF_A", [128, 16384], BF16))
    b.BUF_B = ent(nc.sbuf_tensor("BUF_B", [128, 16384], BF16))
    b.WS = ent(nc.sbuf_tensor("WS", [128, 16384], BF16))
    b.SMALL = ent(nc.sbuf_tensor("SMALL", [128, 768], F32))
    b.SBF = ent(nc.sbuf_tensor("SBF", [128, 4608], BF16))
    b.CONST = ent(nc.sbuf_tensor("CONST", [128, 576], F32))
    b.CBF = ent(nc.sbuf_tensor("CBF", [128, 160], BF16))
    b.PA = ent(nc.psum_tensor("PA", [128, 2048], F32))
    b.PB = ent(nc.psum_tensor("PB", [128, 1024], F32))
    b.PC = ent(nc.psum_tensor("PC", [128, 512], F32))
    b.PT = ent(nc.psum_tensor("PT", [128, 1024], BF16))
    return b


SM_S, SM_Q, SM_M, SM_R, SM_T, SM_W = 0, 16, 32, 48, 64, 80
SM_GATE = 96
SM_WFIN = 160
SM_X = 672

SB_ONES = 0
SB_BD = 512
SB_BUW = 1024
SB_MISC = 2048
SB_SEL = 2176
SB_BU8 = 2688
SB_JUNK = 3712
SB_BIAS1 = SB_BUW
SB_BIAS2 = SB_SEL

CBF_ONESC = 0
CBF_ID = 1
CB_B8 = 136
C_TRI_LO, C_TRI_HI = 0, 128
C_G = 256
C_QG = 272
C_KG = 273
C_ONE = 274
C_ONER = 280
C_IDF32 = 416
C_SH = 275
C_B2 = 264


def emit_consts(S, b, dram):
    S.dma("gpsimd", b.CBF[:, 0:1], dram["ones128"], "WD")
    S.dma("gpsimd", b.CBF[:, 1:129], dram["ident"], "WD")
    S.dma("gpsimd", b.SBF[0:1, 0:512], dram["ones512r"], "WD")
    S.dma("sync", b.CONST[:, 0:128], dram["tri_lo"], "XD")
    S.dma("sync", b.CONST[:, 128:256], dram["tri_hi"], "XD")
    S.dma("sync", b.CONST[:, C_ONE:C_ONE + 1], dram["ones128"], "XD")
    S.dma("sync", b.CONST[0:1, C_ONER:C_ONER + 128], dram["ones512r"][0:1, 0:128], "XD")
    S.dma("sync", b.CONST[:, C_IDF32:C_IDF32 + 128], dram["ident"], "XD")




def emit_ln0(S, b, out_tok=None, out_fm=True, scale_cols=None, post_stats=None):
    if out_tok is None:
        out_tok = b.BUF_A[:, 0:NT * D]
    S.call("vector", "memset", b.SMALL[:, SM_Q:SM_Q + NT], 0.0, inc=("VE", 1))
    S.wait("scalar", "VE")
    for i in range(NT):
        x = b.XRES[:, i * D:(i + 1) * D]
        S.call("vector", "reduce_sum", out=b.SMALL[:, SM_S + i:SM_S + i + 1],
               in_=x, axis=AX.X,
               inc=(("VE", 1) if i == NT - 1 else None))
        S.call("scalar", "activation", out=out_tok[:, i * D:(i + 1) * D],
               in_=x, func=AF.Square,
               accum_out=b.SMALL[:, SM_Q + i:SM_Q + i + 1],
               inc=(("AC", 1) if i == NT - 1 else None))
    S.wait("vector", "VE")
    sS = b.SMALL[:, SM_S:SM_S + NT]
    sQ = b.SMALL[:, SM_Q:SM_Q + NT]
    sM = b.SMALL[:, SM_M:SM_M + NT]
    sR = b.SMALL[:, SM_R:SM_R + NT]
    S.chained("vector", "VE", "tensor_scalar", out=sM, in0=sS, scalar1=1.0 / D,
              scalar2=None, op0=ALU.mult)
    S.wait("vector", "AC")
    S.chained("vector", "VE", "tensor_scalar", out=sQ, in0=sQ, scalar1=1.0 / D,
              scalar2=None, op0=ALU.mult)
    S.chained("vector", "VE", "scalar_tensor_tensor", out=sR, in0=sM, scalar=1.0,
              in1=sM, op0=ALU.mult, op1=ALU.mult)
    S.chained("vector", "VE", "tensor_tensor", out=sR, in0=sR, in1=sQ,
              op=ALU.subtract)
    S.chained("vector", "VE", "tensor_scalar", out=sR, in0=sR, scalar1=-1.0,
              scalar2=LN_EPS, op0=ALU.mult, op1=ALU.add)
    S.chained("vector", "VE", "tensor_copy", out=b.SMALL[:, SM_W:SM_W + NT], in_=sR)
    S.wait("scalar", "VE")
    S.chained("scalar", "AC", "activation", out=sR, in_=sR, func=AF.Ln,
              scale=1.0)
    S.call("scalar", "activation", out=sR, in_=sR, func=AF.Exp,
           scale=-0.5, inc=("AC", 1))
    S.wait("vector", "AC")
    if post_stats is not None:
        post_stats()
    for i in range(NT):
        x = b.XRES[:, i * D:(i + 1) * D]
        sc2 = sR[:, i:i + 1] if scale_cols is None else scale_cols[:, i:i + 1]
        S.call("vector", "tensor_scalar", out=out_tok[:, i * D:(i + 1) * D],
               in0=x, scalar1=sM[:, i:i + 1], scalar2=sc2,
               op0=ALU.subtract, op1=ALU.mult,
               inc=(("VE", 1) if i == NT - 1 else None))
    if out_fm:
        emit_tok2fm(S, b, out_tok, b.XFM)


def emit_tok2fm(S, b, src_tok, dst_fm):
    ident = b.CBF[:, 1:129]
    dst = dst_fm if not hasattr(dst_fm, "ap") else dst_fm
    for i in range(NT):
        S.wait("tensor", "VE")
        trs = [(b.PT[:, c * 128:(c + 1) * 128],
                src_tok[:, i * D + c * 128: i * D + (c + 1) * 128])
               for c in range(NC_)]
        S.transposes(trs, ident, ("PE", 1))
        S.wait("vector", "PE")
        S.call("vector", "tensor_copy",
               out=dst.rearrange("p (c t) -> p c t", c=NC_)[:, :, i * 128:(i + 1) * 128],
               in_=b.PT[:].rearrange("p (c t) -> p c t", c=NC_, t=128),
               inc=("VE", 1))




def emit_ffn0(S, b, dram):
    wd = b.WS[:, 0:NC_ * 512]
    wu = b.WS[:, 8192:8192 + 4 * 1024]
    S.wait("gpsimd", "PE")
    S.dma("gpsimd", wd.rearrange("p (c i) -> p c i", c=NC_),
          dram["f0_WdT"].rearrange("(c p) i -> p c i", p=128), "WD")
    S.dma("gpsimd", wu.rearrange("p (c h) -> p c h", c=4),
          dram["f0_WuT"].rearrange("(c p) h -> p c h", p=128), "WD")
    S.dma("gpsimd", b.SBF[0:1, SB_BD:SB_BD + 512], dram["f0_bd"][None, :], "WD")
    S.dma("gpsimd", b.SBF[0:8, SB_SEL:SB_SEL + 512], dram["f0_SEL"], "WD")
    S.dma("gpsimd", b.SBF[0:8, SB_BU8:SB_BU8 + 1024], dram["f0_bu"], "WD")
    gwf = b.BUF_B[0:1, 0:16384].bitcast(F32)
    S.wait("sync", "PE")
    S.dma("sync", gwf, dram["f0_gw_flat"], "XD")

    S.wait("tensor", "VE")
    ones_col = b.CBF[:, 0:1]
    grpA = [(b.PC[0:1, 0:512], ones_col, b.BUF_A[:, i * D: i * D + 512])
            for i in range(NT)]
    grpB = [(b.PB[0:1, 0:512], ones_col, b.BUF_A[:, i * D + 512: (i + 1) * D])
            for i in range(NT)]
    S.mm_group(grpA, ("PE", 1))
    S.mm_group(grpB, ("PE", 1))

    S.wait("vector", "PE")
    S.wait("vector", "XD")
    junk = b.SBF[0:1, SB_JUNK:SB_JUNK + 512]
    lA = b.SMALL[0:1, SM_GATE + 40:SM_GATE + 48]
    lB = b.SMALL[0:1, SM_GATE + 48:SM_GATE + 56]
    S.chained("vector", "VE", "memset", b.SMALL[0:1, SM_GATE + 40:SM_GATE + 56], 0.0)
    for e in range(8):
        S.call("vector", "scalar_tensor_tensor", out=junk[0:1, 0:512],
               in0=b.PC[0:1, 0:512], scalar=1.0,
               in1=gwf[0:1, e * 1024:e * 1024 + 512],
               op0=ALU.mult, op1=ALU.mult, accum_out=lA[0:1, e:e + 1],
               inc=(("VE", 1) if e == 7 else None))
        S.call("vector", "scalar_tensor_tensor", out=junk[0:1, 0:512],
               in0=b.PB[0:1, 0:512], scalar=1.0,
               in1=gwf[0:1, e * 1024 + 512:(e + 1) * 1024],
               op0=ALU.mult, op1=ALU.mult, accum_out=lB[0:1, e:e + 1],
               inc=(("VE", 1) if e == 7 else None))
    S.wait("vector", "VE")
    logits = b.SMALL[0:1, SM_GATE:SM_GATE + 8]
    S.chained("vector", "VE", "tensor_tensor", out=logits, in0=lA, in1=lB, op=ALU.add)

    c0 = b.SMALL[0:1, SM_GATE + 8:SM_GATE + 9]
    c1 = b.SMALL[0:1, SM_GATE + 9:SM_GATE + 10]
    c2 = b.SMALL[0:1, SM_GATE + 10:SM_GATE + 11]
    g = b.SMALL[0:1, SM_GATE + 16:SM_GATE + 24]
    eq1 = b.SMALL[0:1, SM_GATE + 24:SM_GATE + 32]
    eq2 = b.SMALL[0:1, SM_GATE + 32:SM_GATE + 40]
    wrow = b.SMALL[0:1, SM_GATE + 56:SM_GATE + 64]
    S.chained("vector", "VE", "reduce_max", out=c0, in_=logits, axis=AX.X)
    S.chained("vector", "VE", "tensor_scalar", out=c0, in0=c0, scalar1=-1.0 / 2048.0,
              scalar2=None, op0=ALU.mult)
    S.wait("scalar", "VE")
    S.chained("scalar", "AC", "activation", out=g, in_=logits, func=AF.Exp,
              scale=1.0 / 2048.0, bias=c0)
    S.wait("vector", "AC")
    S.chained("vector", "VE", "reduce_sum", out=c1, in_=g, axis=AX.X)
    S.chained("vector", "VE", "reciprocal", out=c1, in_=c1)
    S.chained("vector", "VE", "tensor_scalar", out=g, in0=g, scalar1=c1, scalar2=None,
              op0=ALU.mult)
    S.chained("vector", "VE", "reduce_max", out=c0, in_=g, axis=AX.X)
    S.chained("vector", "VE", "tensor_scalar", out=eq1, in0=g, scalar1=c0,
              scalar2=None, op0=ALU.is_equal)
    S.chained("vector", "VE", "scalar_tensor_tensor", out=eq2, in0=eq1, scalar=-1e30,
              in1=g, op0=ALU.mult, op1=ALU.add)
    S.chained("vector", "VE", "reduce_max", out=c1, in_=eq2, axis=AX.X)
    S.chained("vector", "VE", "tensor_scalar", out=eq2, in0=eq2, scalar1=c1,
              scalar2=None, op0=ALU.is_equal)
    S.chained("vector", "VE", "tensor_tensor", out=c2, in0=c0, in1=c1, op=ALU.add)
    S.chained("vector", "VE", "tensor_scalar", out=c2, in0=c2, scalar1=1e-8,
              scalar2=None, op0=ALU.add)
    S.chained("vector", "VE", "reciprocal", out=c2, in_=c2)
    S.chained("vector", "VE", "tensor_scalar", out=c0, in0=c0, scalar1=c2,
              scalar2=None, op0=ALU.mult)
    S.chained("vector", "VE", "tensor_scalar", out=c1, in0=c1, scalar1=c2,
              scalar2=None, op0=ALU.mult)
    S.chained("vector", "VE", "tensor_scalar", out=eq2, in0=eq2, scalar1=c1,
              scalar2=None, op0=ALU.mult)
    S.chained("vector", "VE", "scalar_tensor_tensor", out=wrow, in0=eq1, scalar=c0,
              in1=eq2, op0=ALU.mult, op1=ALU.add)
    wrow_bf = b.SBF[0:1, SB_MISC:SB_MISC + 8]
    S.call("vector", "tensor_copy", out=wrow_bf, in_=wrow, inc=("VE", 1))

    S.wait("tensor", "VE")
    S.transposes([(b.PT[0:8, 512:513], wrow_bf)], b.CBF[0:1, 1:2], ("PE", 1))
    S.wait("vector", "PE")
    wrowT = b.SBF[0:8, SB_MISC + 8:SB_MISC + 9]
    S.call("vector", "tensor_copy", out=wrowT, in_=b.PT[0:8, 512:513],
           inc=("VE", 1))
    S.wait("tensor", "VE")
    S.wait("tensor", "WD")
    for c in range(4):
        S.mm_group([(b.PC[:, 16 + c:17 + c],
                     b.SBF[0:8, SB_SEL + c * 128:SB_SEL + (c + 1) * 128],
                     wrowT)], ("PE", 1))
    S.mm_group([(b.PB[0:1, 0:512], wrowT,
                 b.SBF[0:8, SB_BU8:SB_BU8 + 512])], ("PE", 1))
    S.mm_group([(b.PB[0:1, 512:1024], wrowT,
                 b.SBF[0:8, SB_BU8 + 512:SB_BU8 + 1024])], ("PE", 1))
    S.wait("vector", "PE")
    wcol = b.SMALL[:, SM_WFIN:SM_WFIN + 4]
    S.call("vector", "tensor_copy", out=wcol, in_=b.PC[:, 16:20])
    buw = b.SBF[0:1, SB_BUW:SB_BUW + 1024]
    S.call("vector", "tensor_copy", out=buw[0:1, 0:512], in_=b.PB[0:1, 0:512])
    S.call("vector", "tensor_copy", out=buw[0:1, 512:1024], in_=b.PB[0:1, 512:1024])
    for c in range(4):
        S.call("vector", "tensor_scalar", out=wu[:, c * 1024:(c + 1) * 1024],
               in0=wu[:, c * 1024:(c + 1) * 1024], scalar1=wcol[:, c:c + 1],
               scalar2=None, op0=ALU.mult,
               inc=(("VE", 1) if c == 3 else None))

    h_bf = b.BUF_B[:, 0:4 * SEQ]
    bd_row = b.SBF[0:1, SB_BD:SB_BD + 512]
    ones_row = b.SBF[0:1, SB_ONES:SB_ONES + 512]
    S.wait("tensor", "VE")
    for ts in range(4):
        for c in range(4):
            grp = [(b.PA[:, c * 512:(c + 1) * 512],
                    wd[:, hc * 512 + c * 128: hc * 512 + (c + 1) * 128],
                    b.XFM[:, hc * SEQ + ts * 512: hc * SEQ + (ts + 1) * 512])
                   for hc in range(NC_)]
            grp.append((b.PA[:, c * 512:(c + 1) * 512],
                        bd_row[0:1, c * 128:(c + 1) * 128], ones_row))
            S.mm_group(grp, ("PE", 1))
        S.wait("scalar", "PE")
        S.call("scalar", "activation",
               out=h_bf.rearrange("p (c t) -> p c t", c=4)[:, :, ts * 512:(ts + 1) * 512],
               in_=b.PA[:].rearrange("p (c t) -> p c t", c=4),
               func=AF.Gelu, inc=("AC", 1))
        S.wait("tensor", "AC")
        for j in range(4):
            tt = ts * 4 + j
            for h in range(2):
                grp = [(b.PB[:, h * 512:(h + 1) * 512],
                        h_bf[:, c * SEQ + tt * 128: c * SEQ + (tt + 1) * 128],
                        wu[:, c * 1024 + h * 512: c * 1024 + (h + 1) * 512])
                       for c in range(4)]
                grp.append((b.PB[:, h * 512:(h + 1) * 512],
                            ones_row[0:1, 0:128],
                            buw[0:1, h * 512:(h + 1) * 512]))
                S.mm_group(grp, ("PE", 1))
            S.wait("vector", "PE")
            S.call("vector", "tensor_tensor",
                   out=b.XRES[:, tt * D:(tt + 1) * D],
                   in0=b.XRES[:, tt * D:(tt + 1) * D],
                   in1=b.PB[:, 0:1024], op=ALU.add, inc=("VE", 1))
            S.wait("tensor", "VE")




def emit_shift_col(S, b, shift):
    S.chained("vector", "VE", "tensor_scalar", out=b.CONST[:, C_SH:C_SH + 1],
              in0=b.CONST[:, C_ONE:C_ONE + 1], scalar1=float(-shift * 0.125),
              scalar2=None, op0=ALU.mult)
    S.wait("scalar", "VE")


def emit_attn_heads(S, b, qcols, kw_lo, kw_hi, shift, band_tt=None):
    ones_col = b.CBF[:, 0:1]
    XQ = b.XFM[:, 3072:3072 + NC_ * qcols]
    XP = [b.XFM[:, 7168:7168 + 2 * qcols], b.XFM[:, 12800:12800 + 2 * qcols]]
    XO = b.XFM[:, 8192:8192 + NC_ * qcols]
    REP = b.XFM[:, 12288:12288 + qcols]
    rrow = b.SMALL[0:1, SM_T:SM_T + 1].bitcast(F32) if False else None
    rrowf = b.SMALL[0:1, 96:96 + qcols]
    rrow_bf = b.SBF[0:1, SB_JUNK:SB_JUNK + qcols]
    nkt = kw_hi - kw_lo + 1
    for fc in range(8):

        def emit_scores(j):
            kt = kw_lo + j
            bk = (j % 2) * 2
            S.wait("tensor", "AC")
            for po in (0, 64):
                S.call("tensor", "matmul",
                       out=b.PA[:, (bk + po // 64) * 512:(bk + po // 64) * 512 + qcols],
                       lhsT=b.BUF_B[po:po + 64, fc * SEQ + kt * 128: fc * SEQ + (kt + 1) * 128],
                       rhs=XQ[po:po + 64, fc * qcols:(fc + 1) * qcols],
                       start=True, stop=True,
                       inc=(("PE", 1) if po else None))
            if band_tt is not None:
                d = kt - band_tt
                if d != 0:
                    S.wait("vector", "PE")
                    msk = C_TRI_HI if d == 1 else C_TRI_LO
                    for po in (0, 64):
                        S.chained("vector", "VE", "tensor_tensor",
                                  out=b.PA[:, (bk + po // 64) * 512:(bk + po // 64) * 512 + qcols],
                                  in0=b.PA[:, (bk + po // 64) * 512:(bk + po // 64) * 512 + qcols],
                                  in1=b.CONST[:, msk:msk + qcols], op=ALU.add)

        def emit_exp(j):
            bk = (j % 2) * 2
            if band_tt is not None:
                S.wait("scalar", "VE")
            S.wait("scalar", "PE")
            if qcols == 512:
                S.call("scalar", "activation", out=XP[j % 2],
                       in_=b.PA[:, bk * 512:bk * 512 + 1024],
                       func=AF.Exp, scale=0.125, bias=b.CONST[:, C_SH:C_SH + 1],
                       inc=("AC", 1))
            else:
                for po in (0, 64):
                    S.call("scalar", "activation",
                           out=XP[j % 2][:, (po // 64) * qcols:(po // 64 + 1) * qcols],
                           in_=b.PA[:, (bk + po // 64) * 512:(bk + po // 64) * 512 + qcols],
                           func=AF.Exp, scale=0.125, bias=b.CONST[:, C_SH:C_SH + 1],
                           inc=(("AC", 1) if po else None))

        def xp_slice(j, po):
            return XP[j % 2][:, (po // 64) * qcols:(po // 64 + 1) * qcols]

        def emit_out(j):
            kt = kw_lo + j
            if j == 0:
                S.wait("tensor", "VE")
            S.wait("tensor", "AC")
            for po in (0, 64):
                S.call("tensor", "matmul", out=b.PB[po:po + 64, 0:qcols],
                       lhsT=b.BUF_A[:, kt * D + (fc * 2 + po // 64) * 64:
                                    kt * D + (fc * 2 + po // 64) * 64 + 64],
                       rhs=xp_slice(j, po), start=(j == 0), stop=(j == nkt - 1),
                       inc=("PE", 1))
                S.call("tensor", "matmul", out=b.PC[po:po + 1, 0:qcols],
                       lhsT=ones_col, rhs=xp_slice(j, po), start=(j == 0),
                       stop=(j == nkt - 1), inc=("PE", 1))

        for j in range(nkt):
            emit_scores(j)
            if j >= 1:
                emit_exp(j - 1)
                emit_out(j - 1)
        emit_exp(nkt - 1)
        emit_out(nkt - 1)
        S.wait("vector", "PE")
        for po in (0, 64):
            S.chained("vector", "VE", "tensor_copy", out=rrowf[0:1, 0:qcols],
                      in_=b.PC[po:po + 1, 0:qcols])
            S.chained("vector", "VE", "reciprocal", out=rrowf[0:1, 0:qcols],
                      in_=rrowf[0:1, 0:qcols])
            S.chained("vector", "VE", "tensor_copy", out=rrow_bf[0:1, 0:qcols],
                      in_=rrowf[0:1, 0:qcols])
            S.wait("tensor", "VE")
            PTf2 = b.PT[:].bitcast(F32)
            S.call("tensor", "matmul", out=PTf2[po:po + 64, 0:qcols],
                   lhsT=b.SBF[0:1, SB_ONES:SB_ONES + 64],
                   rhs=rrow_bf[0:1, 0:qcols], start=True, stop=True, inc=("PE", 1))
            S.wait("vector", "PE")
            S.call("vector", "tensor_copy", out=REP[po:po + 64],
                   in_=PTf2[po:po + 64, 0:qcols], inc=("VE", 1))
            S.wait("vector", "VE")
            S.call("vector", "tensor_tensor",
                   out=XO[po:po + 64, fc * qcols:(fc + 1) * qcols],
                   in0=b.PB[po:po + 64, 0:qcols], in1=REP[po:po + 64], op=ALU.mult,
                   inc=("VE", 1))


def emit_mha(S, b, dram, pfx, extras=None):
    slot0 = b.WS[:, 0:8192]
    slot1 = b.WS[:, 8192:16384]
    ones_row = b.SBF[0:1, SB_ONES:SB_ONES + 512]
    ident = b.CBF[:, 1:129]
    bias1 = b.SBF[0:1, SB_BIAS1:SB_BIAS1 + 1024]
    bias2 = b.SBF[0:1, SB_BIAS2:SB_BIAS2 + 1024]

    S.wait("gpsimd", "PE")
    S.dma("gpsimd", slot0.rearrange("p (c f) -> p c f", c=NC_),
          dram[pfx + "_Wk"].rearrange("(c p) f -> p c f", p=128), "WD")
    S.dma("gpsimd", slot1.rearrange("p (c f) -> p c f", c=NC_),
          dram[pfx + "_Wv"].rearrange("(c p) f -> p c f", p=128), "WD")
    S.dma("gpsimd", bias1, dram[pfx + "_bk"][None, :], "WD")
    S.dma("gpsimd", bias2, dram[pfx + "_bv"][None, :], "WD")
    S.wait("tensor", "WD")
    S.wait("tensor", "VE")

    ve_hist = []
    for fc in range(NC_):
        for half in range(2):
            if len(ve_hist) >= 2:
                S.wait("tensor", "VE", val=ve_hist[-2])
            for ts2 in range(2):
                ts = half * 2 + ts2
                grp = [(b.PA[:, half * 1024 + ts2 * 512: half * 1024 + (ts2 + 1) * 512],
                        slot0[:, cin * 1024 + fc * 128: cin * 1024 + (fc + 1) * 128],
                        b.XFM[:, cin * SEQ + ts * 512: cin * SEQ + (ts + 1) * 512])
                       for cin in range(NC_)]
                grp.append((b.PA[:, half * 1024 + ts2 * 512: half * 1024 + (ts2 + 1) * 512],
                            bias1[0:1, fc * 128:(fc + 1) * 128], ones_row))
                S.mm_group(grp, ("PE", 1))
            S.wait("vector", "PE")
            S.call("vector", "tensor_copy",
                   out=b.BUF_B[:, fc * SEQ + half * 1024: fc * SEQ + (half + 1) * 1024],
                   in_=b.PA[:, half * 1024:(half + 1) * 1024],
                   inc=("VE", 1))
            ve_hist.append(S.cnt["VE"])
    for tt in range(NT):
        if len(ve_hist) >= 2:
            S.wait("tensor", "VE", val=ve_hist[-2])
        hb = (tt % 2) * 1024
        for h2 in range(2):
            grp = [(b.PA[:, hb + h2 * 512: hb + (h2 + 1) * 512],
                    b.XFM[:, cin * SEQ + tt * 128: cin * SEQ + (tt + 1) * 128],
                    slot1[:, cin * 1024 + h2 * 512: cin * 1024 + (h2 + 1) * 512])
                   for cin in range(NC_)]
            grp.append((b.PA[:, hb + h2 * 512: hb + (h2 + 1) * 512],
                        ones_row[0:1, 0:128], bias2[0:1, h2 * 512:(h2 + 1) * 512]))
            S.mm_group(grp, ("PE", 1))
        S.wait("vector", "PE")
        S.call("vector", "tensor_copy",
               out=b.BUF_A[:, tt * D:(tt + 1) * D], in_=b.PA[:, hb:hb + 1024],
               inc=("VE", 1))
        ve_hist.append(S.cnt["VE"])
    S.wait("tensor", "VE", val=ve_hist[-1])

    S.wait("gpsimd", "PE")
    S.dma("gpsimd", slot0.rearrange("p (c f) -> p c f", c=NC_),
          dram[pfx + "_Wq"].rearrange("(c p) f -> p c f", p=128), "WD")
    S.dma("gpsimd", slot1.rearrange("p (c f) -> p c f", c=NC_),
          dram[pfx + "_Wo"].rearrange("(c p) f -> p c f", p=128), "WD")
    S.wait("vector", "PE")
    S.dma("gpsimd", bias1, dram[pfx + "_bq"][None, :], "WD")
    S.dma("gpsimd", bias2, dram[pfx + "_bo"][None, :], "WD")
    S.wait("tensor", "WD")

    XT = b.XFM[:, 0:1024]
    XQF = b.XFM[:, 1024:2048]
    XQ = b.XFM[:, 3072:3072 + NC_ * 512]
    XO = b.XFM[:, 8192:8192 + NC_ * 512]
    sM = b.SMALL[:, SM_M:SM_M + NT]
    sR = b.SMALL[:, SM_R:SM_R + NT]
    emit_shift_col(S, b, 16.0)
    for sl in range(4):
        for tl in range(4):
            tt = sl * 4 + tl
            S.wait("vector", "PE")
            S.call("vector", "tensor_scalar", out=XT, in0=b.XRES[:, tt * D:(tt + 1) * D],
                   scalar1=sM[:, tt:tt + 1], scalar2=sR[:, tt:tt + 1],
                   op0=ALU.subtract, op1=ALU.mult, inc=("VE", 1))
            S.wait("tensor", "VE")
            S.transposes([(b.PT[:, c * 128:(c + 1) * 128], XT[:, c * 128:(c + 1) * 128])
                          for c in range(NC_)], ident, ("PE", 1))
            S.wait("vector", "PE")
            S.call("vector", "tensor_copy", out=XQF, in_=b.PT[:], inc=("VE", 1))
            S.wait("tensor", "VE")
            for fc in range(NC_):
                grp = [(b.PA[:, fc * 128:(fc + 1) * 128],
                        slot0[:, cin * 1024 + fc * 128: cin * 1024 + (fc + 1) * 128],
                        XQF[:, cin * 128:(cin + 1) * 128]) for cin in range(NC_)]
                grp.append((b.PA[:, fc * 128:(fc + 1) * 128],
                            bias1[0:1, fc * 128:(fc + 1) * 128], ones_row[0:1, 0:128]))
                S.mm_group(grp, ("PE", 1))
            S.wait("vector", "PE")
            S.call("vector", "tensor_copy",
                   out=XQ.rearrange("p (c q) -> p c q", c=NC_)[:, :, tl * 128:(tl + 1) * 128],
                   in_=b.PA[:, 0:1024].rearrange("p (c q) -> p c q", c=NC_),
                   inc=("VE", 1))
            S.wait("tensor", "VE")
        emit_attn_heads(S, b, 512, 0, NT - 1, 16.0)
        S.wait("tensor", "VE")
        op_hist = []
        for tl in range(4):
            tt = sl * 4 + tl
            hb = (tl % 2) * 1024
            if len(op_hist) >= 2:
                S.wait("tensor", "VE", val=op_hist[-2])
            for h2 in range(2):
                grp = [(b.PA[:, hb + h2 * 512: hb + (h2 + 1) * 512],
                        XO[:, cin * 512 + tl * 128: cin * 512 + tl * 128 + 128],
                        slot1[:, cin * 1024 + h2 * 512: cin * 1024 + (h2 + 1) * 512])
                       for cin in range(NC_)]
                grp.append((b.PA[:, hb + h2 * 512: hb + (h2 + 1) * 512],
                            ones_row[0:1, 0:128], bias2[0:1, h2 * 512:(h2 + 1) * 512]))
                S.mm_group(grp, ("PE", 1))
            S.wait("vector", "PE")
            S.call("vector", "tensor_tensor", out=b.XRES[:, tt * D:(tt + 1) * D],
                   in0=b.XRES[:, tt * D:(tt + 1) * D], in1=b.PA[:, hb:hb + 1024],
                   op=ALU.add, inc=("VE", 1))
            op_hist.append(S.cnt["VE"])
        S.wait("tensor", "VE", val=op_hist[-1])


SMX_S = SM_X
SMX_C1 = SM_X + 16
SMX_C2 = SM_X + 32
SMX_F = SM_X + 48


def emit_ln2cols(S, b, want_resid=False):
    sA = b.SMALL[:, SM_W:SM_W + NT]
    sR = b.SMALL[:, SM_R:SM_R + NT]
    sM = b.SMALL[:, SM_M:SM_M + NT]
    sS2 = b.SMALL[:, SMX_S:SMX_S + NT]
    S.chained("vector", "VE", "reciprocal", out=sS2, in_=sA)
    S.chained("vector", "VE", "tensor_scalar", out=sS2, in0=sS2,
              scalar1=-LN_EPS, scalar2=1.0 + LN_EPS, op0=ALU.mult, op1=ALU.add)
    S.wait("scalar", "VE")
    S.chained("scalar", "AC", "activation", out=sS2, in_=sS2, func=AF.Ln, scale=1.0)
    S.call("scalar", "activation", out=sS2, in_=sS2, func=AF.Exp,
           scale=-0.5, inc=("AC", 1))
    S.wait("vector", "AC")
    S.chained("vector", "VE", "tensor_tensor", out=sS2, in0=sS2, in1=sR,
              op=ALU.mult)
    if want_resid:
        S.chained("vector", "VE", "tensor_scalar",
                  out=b.SMALL[:, SMX_C1:SMX_C1 + NT], in0=sR, scalar1=1.0,
                  scalar2=None, op0=ALU.add)
        S.chained("vector", "VE", "tensor_tensor",
                  out=b.SMALL[:, SMX_C2:SMX_C2 + NT], in0=sM, in1=sR, op=ALU.mult)


def _emit_rms(S, b, tmp, junk, qg_col=None, eighth_bias=None):
    fcol = b.SMALL[:, SMX_F:SMX_F + 16]
    S.chained("vector", "VE", "memset", fcol, 0.0)
    for h in range(16):
        sl = tmp[:, h * 64:(h + 1) * 64]
        S.call("vector", "scalar_tensor_tensor", out=junk[:, 0:64], in0=sl,
               scalar=1.0, in1=sl, op0=ALU.mult, op1=ALU.mult,
               accum_out=fcol[:, h:h + 1],
               inc=(("VE", 1) if h == 15 else None))
    S.wait("vector", "VE")
    S.wait("scalar", "VE")
    S.chained("scalar", "AC", "activation", out=fcol, in_=fcol, func=AF.Ln, scale=1.0)
    S.call("scalar", "activation", out=fcol, in_=fcol, func=AF.Exp,
           scale=0.5, inc=("AC", 1))
    S.wait("vector", "AC")
    S.chained("vector", "VE", "tensor_scalar", out=fcol, in0=fcol,
              scalar1=0.125, scalar2=1e-8, op0=ALU.mult, op1=ALU.max)
    S.chained("vector", "VE", "reciprocal", out=fcol, in_=fcol)
    for h in range(16):
        sl = tmp[:, h * 64:(h + 1) * 64]
        S.call("vector", "tensor_scalar", out=sl, in0=sl,
               scalar1=fcol[:, h:h + 1], scalar2=None, op0=ALU.mult,
               inc=(("VE", 1) if h == 15 else None))
    S.wait("vector", "VE")


def emit_mha2(S, b, dram, pfx, rms=False, band=False, inner_beta=True,
              out_bias=True, inner_residual=False, stop_after=None):
    slot0 = b.WS[:, 0:8192]
    slot1 = b.WS[:, 8192:16384]
    ones_row = b.SBF[0:1, SB_ONES:SB_ONES + 512]
    ident = b.CBF[:, 1:129]
    bias1 = b.SBF[0:1, SB_BIAS1:SB_BIAS1 + 1024]
    bias2 = b.SBF[0:1, SB_BIAS2:SB_BIAS2 + 1024]
    gcol = b.CONST[:, C_G:C_G + 8]
    bcol = b.CBF[:, CB_B8:CB_B8 + 8]
    junk = b.SBF[:, SB_JUNK:SB_JUNK + 512]

    def load_fold(wname, bias_row, with_beta, with_gamma):
        slot = slot0 if wname in ("Wk", "Wq") else slot1
        S.wait("gpsimd", "PE")
        S.wait("gpsimd", "VE")
        S.dma("gpsimd", slot.rearrange("p (c f) -> p c f", c=NC_),
              dram[pfx + "_" + wname].rearrange("(c p) f -> p c f", p=128), "WD")
        S.wait("tensor", "WD")
        if with_beta:
            for hf in range(2):
                grp = [(b.PC[0:1, 0:512],
                        bcol[:, c:c + 1],
                        slot[:, c * 1024 + hf * 512: c * 1024 + (hf + 1) * 512])
                       for c in range(NC_)]
                S.mm_group(grp, ("PE", 1))
                S.wait("vector", "PE")
                S.call("vector", "tensor_copy",
                       out=bias_row[0:1, hf * 512:(hf + 1) * 512],
                       in_=b.PC[0:1, 0:512], inc=("VE", 1))
                S.wait("tensor", "VE")
        if with_gamma:
            for c in range(NC_):
                S.call("vector", "tensor_scalar",
                       out=slot[:, c * 1024:(c + 1) * 1024],
                       in0=slot[:, c * 1024:(c + 1) * 1024],
                       scalar1=gcol[:, c:c + 1], scalar2=None, op0=ALU.mult,
                       inc=(("VE", 1) if c == NC_ - 1 else None))
            S.wait("tensor", "VE")

    load_fold("Wk", bias1, inner_beta, True)
    ktmps = [b.BUF_A[:, 0:1024], b.BUF_A[:, 1024:2048]]
    kv_hist = []
    for kt in range(NT):
        hb = (kt % 2) * 1024
        ktmp = ktmps[kt % 2]
        if len(kv_hist) >= 2:
            S.wait("tensor", "VE", val=kv_hist[-2])
        for h2 in range(2):
            grp = [(b.PA[:, hb + h2 * 512: hb + (h2 + 1) * 512],
                    b.XFM[:, cin * SEQ + kt * 128: cin * SEQ + (kt + 1) * 128],
                    slot0[:, cin * 1024 + h2 * 512: cin * 1024 + (h2 + 1) * 512])
                   for cin in range(NC_)]
            if inner_beta:
                grp.append((b.PA[:, hb + h2 * 512: hb + (h2 + 1) * 512],
                            ones_row[0:1, 0:128],
                            bias1[0:1, h2 * 512:(h2 + 1) * 512]))
            S.mm_group(grp, ("PE", 1))
        S.wait("vector", "PE")
        S.call("vector", "tensor_copy", out=ktmp, in_=b.PA[:, hb:hb + 1024],
               inc=("VE", 1))
        kv_hist.append(S.cnt["VE"])
        if rms:
            _emit_rms(S, b, ktmp, junk)
        S.wait("tensor", "VE")
        S.transposes([(b.PT[:, c * 128:(c + 1) * 128], ktmp[:, c * 128:(c + 1) * 128])
                      for c in range(NC_)], ident, ("PE", 1))
        S.wait("vector", "PE")
        S.call("vector", "tensor_copy",
               out=b.BUF_B.rearrange("p (c t) -> p c t", c=NC_)[:, :, kt * 128:(kt + 1) * 128],
               in_=b.PT[:].rearrange("p (c t) -> p c t", c=NC_, t=128),
               inc=("VE", 1))
    if rms:
        S.wait("vector", "PE")
        S.call("vector", "tensor_scalar", out=b.BUF_B[:], in0=b.BUF_B[:],
               scalar1=b.CONST[:, C_KG:C_KG + 1], scalar2=None, op0=ALU.mult,
               inc=("VE", 1))
    if stop_after == "k":
        return
    load_fold("Wv", bias2, inner_beta, True)
    for tt in range(NT):
        for h2 in range(2):
            grp = [(b.PB[:, h2 * 512:(h2 + 1) * 512],
                    b.XFM[:, cin * SEQ + tt * 128: cin * SEQ + (tt + 1) * 128],
                    slot1[:, cin * 1024 + h2 * 512: cin * 1024 + (h2 + 1) * 512])
                   for cin in range(NC_)]
            if inner_beta:
                grp.append((b.PB[:, h2 * 512:(h2 + 1) * 512],
                            ones_row[0:1, 0:128],
                            bias2[0:1, h2 * 512:(h2 + 1) * 512]))
            S.mm_group(grp, ("PE", 1))
        S.wait("vector", "PE")
        S.call("vector", "tensor_copy", out=b.BUF_A[:, tt * D:(tt + 1) * D],
               in_=b.PB[:], inc=("VE", 1))
        S.wait("tensor", "VE")

    if stop_after == "v":
        return
    load_fold("Wq", bias1, inner_beta, True)
    S.wait("gpsimd", "PE")
    S.dma("gpsimd", slot1.rearrange("p (c f) -> p c f", c=NC_),
          dram[pfx + "_Wo"].rearrange("(c p) f -> p c f", p=128), "WD")
    if out_bias:
        S.wait("gpsimd", "VE")
        S.dma("gpsimd", bias2, dram[pfx + "_bo"][None, :], "WD")
    S.wait("tensor", "WD")

    qcols = 128 if band else 512
    XT = b.XFM[:, 0:1024]
    XQF = b.XFM[:, 1024:2048]
    QTMP = b.XFM[:, 2048:3072]
    XQ = b.XFM[:, 3072:3072 + NC_ * qcols]
    XO = b.XFM[:, 8192:8192 + NC_ * qcols]
    sM = b.SMALL[:, SM_M:SM_M + NT]
    sS2 = b.SMALL[:, SMX_S:SMX_S + NT]
    emit_shift_col(S, b, 60.0 if band else 8.0)
    tiles_per_slab = 1 if band else 4
    for sl in range(NT // tiles_per_slab):
        for tl in range(tiles_per_slab):
            tt = sl * tiles_per_slab + tl
            S.wait("vector", "PE")
            S.call("vector", "tensor_scalar", out=XT, in0=b.XRES[:, tt * D:(tt + 1) * D],
                   scalar1=sM[:, tt:tt + 1], scalar2=sS2[:, tt:tt + 1],
                   op0=ALU.subtract, op1=ALU.mult, inc=("VE", 1))
            S.wait("tensor", "VE")
            S.transposes([(b.PT[:, c * 128:(c + 1) * 128], XT[:, c * 128:(c + 1) * 128])
                          for c in range(NC_)], ident, ("PE", 1))
            S.wait("vector", "PE")
            S.call("vector", "tensor_copy", out=XQF, in_=b.PT[:], inc=("VE", 1))
            S.wait("tensor", "VE")
            for h2 in range(2):
                grp = [(b.PA[:, h2 * 512:(h2 + 1) * 512],
                        XQF[:, cin * 128:(cin + 1) * 128],
                        slot0[:, cin * 1024 + h2 * 512: cin * 1024 + (h2 + 1) * 512])
                       for cin in range(NC_)]
                if inner_beta:
                    grp.append((b.PA[:, h2 * 512:(h2 + 1) * 512],
                                ones_row[0:1, 0:128],
                                bias1[0:1, h2 * 512:(h2 + 1) * 512]))
                S.mm_group(grp, ("PE", 1))
            S.wait("vector", "PE")
            S.call("vector", "tensor_copy", out=QTMP, in_=b.PA[:, 0:1024],
                   inc=("VE", 1))
            S.wait("tensor", "VE")
            if rms:
                _emit_rms(S, b, QTMP, junk)
            S.wait("tensor", "VE")
            S.transposes([(b.PT[:, c * 128:(c + 1) * 128], QTMP[:, c * 128:(c + 1) * 128])
                          for c in range(NC_)], ident, ("PE", 1))
            S.wait("vector", "PE")
            dstq = XQ.rearrange("p (c q) -> p c q", c=NC_)[:, :, tl * 128:(tl + 1) * 128]
            if rms:
                S.call("vector", "tensor_scalar", out=dstq,
                       in0=b.PT[:].rearrange("p (c q) -> p c q", c=NC_, q=128),
                       scalar1=b.CONST[:, C_QG:C_QG + 1], scalar2=None,
                       op0=ALU.mult, inc=("VE", 1))
            else:
                S.call("vector", "tensor_copy", out=dstq,
                       in_=b.PT[:].rearrange("p (c q) -> p c q", c=NC_, q=128),
                       inc=("VE", 1))
            S.wait("tensor", "VE")
        klo = max(0, sl - 1) if band else 0
        khi = min(NT - 1, sl + 1) if band else NT - 1
        emit_attn_heads(S, b, qcols, klo, khi, 60.0 if band else 8.0,
                        band_tt=(sl if band else None))
        S.wait("tensor", "VE")
        op_hist = []
        for tl in range(tiles_per_slab):
            tt = sl * tiles_per_slab + tl
            hb = (tl % 2) * 1024
            if len(op_hist) >= 2:
                S.wait("tensor", "VE", val=op_hist[-2])
            for h2 in range(2):
                grp = [(b.PA[:, hb + h2 * 512: hb + (h2 + 1) * 512],
                        XO[:, cin * qcols + tl * 128: cin * qcols + tl * 128 + 128],
                        slot1[:, cin * 1024 + h2 * 512: cin * 1024 + (h2 + 1) * 512])
                       for cin in range(NC_)]
                if out_bias:
                    grp.append((b.PA[:, hb + h2 * 512: hb + (h2 + 1) * 512],
                                ones_row[0:1, 0:128],
                                bias2[0:1, h2 * 512:(h2 + 1) * 512]))
                S.mm_group(grp, ("PE", 1))
            S.wait("vector", "PE")
            if inner_residual:
                S.chained("vector", "VE", "tensor_scalar",
                          out=b.XRES[:, tt * D:(tt + 1) * D],
                          in0=b.XRES[:, tt * D:(tt + 1) * D],
                          scalar1=b.SMALL[:, SMX_C1 + tt:SMX_C1 + tt + 1],
                          scalar2=b.SMALL[:, SMX_C2 + tt:SMX_C2 + tt + 1],
                          op0=ALU.mult, op1=ALU.subtract)
            S.call("vector", "tensor_tensor", out=b.XRES[:, tt * D:(tt + 1) * D],
                   in0=b.XRES[:, tt * D:(tt + 1) * D], in1=b.PA[:, hb:hb + 1024],
                   op=ALU.add, inc=("VE", 1))
            op_hist.append(S.cnt["VE"])
        if op_hist:
            S.wait("tensor", "VE", val=op_hist[-1])




def emit_aa0_cols(S, b):
    sA = b.SMALL[:, SM_W:SM_W + NT]
    sR = b.SMALL[:, SM_R:SM_R + NT]
    sS2 = b.SMALL[:, SMX_S:SMX_S + NT]
    S.chained("vector", "VE", "reciprocal", out=sS2, in_=sA)
    S.chained("vector", "VE", "tensor_scalar", out=sS2, in0=sS2,
              scalar1=-LN_EPS * D, scalar2=float(D), op0=ALU.mult, op1=ALU.add)
    S.wait("scalar", "VE")
    S.chained("scalar", "AC", "activation", out=sS2, in_=sS2, func=AF.Ln, scale=1.0)
    S.call("scalar", "activation", out=sS2, in_=sS2, func=AF.Exp,
           scale=0.5, inc=("AC", 1))
    S.wait("vector", "AC")
    S.chained("vector", "VE", "tensor_scalar", out=sS2, in0=sS2,
              scalar1=1e-12, scalar2=None, op0=ALU.max)
    S.chained("vector", "VE", "reciprocal", out=sS2, in_=sS2)
    S.chained("vector", "VE", "tensor_tensor", out=sS2, in0=sS2, in1=sR,
              op=ALU.mult)
    S.chained("vector", "VE", "tensor_scalar", out=sS2, in0=sS2,
              scalar1=32.0, scalar2=None, op0=ALU.mult)


def emit_attn_a0(S, b, dram):
    slot0 = b.WS[:, 0:8192]
    slot1 = b.WS[:, 8192:16384]
    ident = b.CBF[:, 1:129]
    ones_col = b.CBF[:, 0:1]
    ones_row = b.SBF[0:1, SB_ONES:SB_ONES + 512]
    gcol = b.CONST[:, C_G:C_G + 8]
    QT = b.BUF_A[:, 0:1024]
    KT = b.BUF_A[:, 1024:2048]
    SQT = b.BUF_A[:, 2048:3072]
    simacc = b.BUF_A[0:64, 3072:5120].bitcast(F32)
    sqq = b.SBF[0:1, 512:2560].bitcast(F32)
    sqk = b.SBF[0:1, 2560:4608].bitcast(F32)

    def load_w(wname, slot, fold):
        S.wait("gpsimd", "PE")
        S.wait("gpsimd", "VE")
        S.dma("gpsimd", slot.rearrange("p (c f) -> p c f", c=NC_),
              dram["aa0_" + wname].rearrange("(c p) f -> p c f", p=128), "WD")
        S.wait("vector", "WD")
        if fold:
            for c in range(NC_):
                S.call("vector", "tensor_scalar",
                       out=slot[:, c * 1024:(c + 1) * 1024],
                       in0=slot[:, c * 1024:(c + 1) * 1024],
                       scalar1=gcol[:, c:c + 1], scalar2=None, op0=ALU.mult,
                       inc=(("VE", 1) if c == NC_ - 1 else None))
        S.wait("tensor", "WD")
        S.wait("tensor", "VE")

    load_w("Wq", slot0, True)
    load_w("Wk", slot1, True)
    S.chained("vector", "VE", "memset", simacc, 0.0)
    S.chained("vector", "VE", "memset", sqq, 0.0)
    S.chained("vector", "VE", "memset", sqk, 0.0)
    S.wait("tensor", "VE")

    for tt in range(NT):
        for h2 in range(2):
            S.mm_group([(b.PB[:, h2 * 512:(h2 + 1) * 512],
                         b.XFM[:, cin * SEQ + tt * 128: cin * SEQ + (tt + 1) * 128],
                         slot0[:, cin * 1024 + h2 * 512: cin * 1024 + (h2 + 1) * 512])
                        for cin in range(NC_)], ("PE", 1))
            S.mm_group([(b.PA[:, h2 * 512:(h2 + 1) * 512],
                         b.XFM[:, cin * SEQ + tt * 128: cin * SEQ + (tt + 1) * 128],
                         slot1[:, cin * 1024 + h2 * 512: cin * 1024 + (h2 + 1) * 512])
                        for cin in range(NC_)], ("PE", 1))
        S.wait("vector", "PE")
        S.call("vector", "tensor_copy", out=QT, in_=b.PB[:])
        S.call("vector", "tensor_copy", out=KT, in_=b.PA[:, 0:1024], inc=("VE", 1))
        S.wait("scalar", "VE")
        S.chained("scalar", "AC", "activation", out=SQT, in_=QT, func=AF.Square)
        S.wait("tensor", "VE")
        for h in range(16):
            S.mm_group([(b.PA[0:64, 1024 + h * 64:1024 + (h + 1) * 64],
                         QT[:, h * 64:(h + 1) * 64], KT[:, h * 64:(h + 1) * 64])],
                       ("PE", 1))
        S.wait("tensor", "AC")
        for hf in range(2):
            S.mm_group([(b.PC[0:1, 0:512], ones_col,
                         SQT[:, hf * 512:(hf + 1) * 512])], ("PE", 1))
            S.wait("vector", "PE")
            S.call("vector", "tensor_tensor", out=sqq[0:1, hf * 512:(hf + 1) * 512],
                   in0=sqq[0:1, hf * 512:(hf + 1) * 512], in1=b.PC[0:1, 0:512],
                   op=ALU.add, inc=("VE", 1))
            S.wait("tensor", "VE")
        S.wait("scalar", "PE")
        S.chained("scalar", "AC", "activation", out=SQT, in_=KT, func=AF.Square)
        S.wait("tensor", "AC")
        for hf in range(2):
            S.mm_group([(b.PC[0:1, 0:512], ones_col,
                         SQT[:, hf * 512:(hf + 1) * 512])], ("PE", 1))
            S.wait("vector", "PE")
            S.call("vector", "tensor_tensor", out=sqk[0:1, hf * 512:(hf + 1) * 512],
                   in0=sqk[0:1, hf * 512:(hf + 1) * 512], in1=b.PC[0:1, 0:512],
                   op=ALU.add, inc=("VE", 1))
            S.wait("tensor", "VE")
        S.wait("vector", "PE")
        S.call("vector", "tensor_tensor", out=simacc, in0=simacc,
               in1=b.PA[0:64, 1024:2048], op=ALU.add, inc=("VE", 1))
        S.wait("tensor", "VE")

    load_w("Wv", slot0, True)
    for fc in range(NC_):
        for ts in range(4):
            S.mm_group([(b.PA[:, ts * 512:(ts + 1) * 512],
                         slot0[:, cin * 1024 + fc * 128: cin * 1024 + (fc + 1) * 128],
                         b.XFM[:, cin * SEQ + ts * 512: cin * SEQ + (ts + 1) * 512])
                        for cin in range(NC_)], ("PE", 1))
        S.wait("vector", "PE")
        S.call("vector", "tensor_copy",
               out=b.BUF_B[:, fc * SEQ:(fc + 1) * SEQ], in_=b.PA[:],
               inc=("VE", 1))
        S.wait("tensor", "VE")
    load_w("Wo", slot1, False)

    S.wait("scalar", "VE")
    S.chained("scalar", "AC", "activation", out=sqk, in_=sqk, func=AF.Ln, scale=1.0)
    S.call("scalar", "activation", out=sqk, in_=sqk, func=AF.Exp,
           scale=0.5, inc=("AC", 1))
    S.chained("scalar", "AC", "activation", out=sqq, in_=sqq, func=AF.Ln, scale=1.0)
    S.call("scalar", "activation", out=sqq, in_=sqq, func=AF.Exp,
           scale=0.5, inc=("AC", 1))
    S.wait("vector", "AC")
    S.chained("vector", "VE", "tensor_scalar", out=sqk, in0=sqk,
              scalar1=1e-12, scalar2=None, op0=ALU.max)
    S.chained("vector", "VE", "reciprocal", out=sqk, in_=sqk)
    S.chained("vector", "VE", "tensor_scalar", out=sqq, in0=sqq,
              scalar1=1e-12, scalar2=None, op0=ALU.max)
    S.chained("vector", "VE", "reciprocal", out=sqq, in_=sqq)
    et = b.SMALL[0:1, SM_T:SM_T + 16]
    S.dma("sync", et, dram["aa0_temp"], "XD")
    S.wait("scalar", "XD")
    S.wait("scalar", "VE")
    S.chained("scalar", "AC", "activation", out=et, in_=et, func=AF.Exp, scale=1.0)
    S.wait("vector", "AC")
    for h in range(16):
        S.call("vector", "tensor_scalar", out=sqk[0:1, h * 64:(h + 1) * 64],
               in0=sqk[0:1, h * 64:(h + 1) * 64], scalar1=et[0:1, h:h + 1],
               scalar2=8.0, op0=ALU.mult, op1=ALU.mult,
               inc=(("VE", 1) if h == 15 else None))
    S.wait("vector", "VE")
    PTf = b.PT[:].bitcast(F32)
    S.wait("tensor", "VE")
    S.transposes([(PTf[0:64, h:h + 1], sqq[0:1, h * 64:(h + 1) * 64])
                  for h in range(16)], b.CONST[0:1, C_ONE:C_ONE + 1], ("PE", 1))
    S.wait("vector", "PE")
    rqP = b.SMALL[0:64, SM_GATE:SM_GATE + 16]
    S.call("vector", "tensor_copy", out=rqP, in_=PTf[0:64, 0:16], inc=("VE", 1))
    XAT = b.XFM[:, 0:2048]
    mx = b.SMALL[:, SM_S:SM_S + 1]
    mxn = b.SMALL[:, SM_S + 1:SM_S + 2]
    rs = b.SMALL[:, SM_S + 2:SM_S + 3]
    rr = b.SMALL[:, SM_S + 3:SM_S + 4]
    prb = b.XFM[:, 2048:2112]
    for h in range(16):
        xcol = (h // 2) * 64 + (h % 2) * 512
        sl = simacc[0:64, h * 64:(h + 1) * 64]
        S.wait("vector", "AC")
        S.chained("vector", "VE", "tensor_scalar", out=sl, in0=sl,
                  scalar1=rqP[0:64, h:h + 1], scalar2=None, op0=ALU.mult)
        S.wait("tensor", "VE")
        S.mm_group([(b.PC[0:64, 0:64], b.CONST[0:1, C_ONER:C_ONER + 64],
                     sqk[0:1, h * 64:(h + 1) * 64])], ("PE", 1))
        S.wait("vector", "PE")
        S.chained("vector", "VE", "tensor_tensor", out=sl, in0=sl,
                  in1=b.PC[0:64, 0:64], op=ALU.mult)
        S.chained("vector", "VE", "reduce_max", out=mx[0:64], in_=sl, axis=AX.X)
        S.chained("vector", "VE", "tensor_scalar", out=mxn[0:64],
                  in0=mx[0:64], scalar1=-1.0, scalar2=None, op0=ALU.mult)
        S.chained("vector", "VE", "memset", rs[0:64], 0.0)
        S.wait("scalar", "VE")
        S.call("scalar", "activation", out=prb[0:64], in_=sl, func=AF.Exp,
               scale=1.0, bias=mxn[0:64], accum_out=rs[0:64], inc=("AC", 1))
        S.wait("vector", "AC")
        S.chained("vector", "VE", "reciprocal", out=rr[0:64], in_=rs[0:64])
        S.chained("vector", "VE", "tensor_scalar", out=prb[0:64],
                  in0=prb[0:64], scalar1=rr[0:64], scalar2=None, op0=ALU.mult)
        S.wait("tensor", "VE")
        S.transposes([(b.PT[0:64, 0:64], prb[0:64])], ident[0:64, 0:64], ("PE", 1))
        S.wait("vector", "PE")
        S.call("vector", "tensor_copy", out=XAT[0:64, xcol:xcol + 64],
               in_=b.PT[0:64, 0:64], inc=("VE", 1))
        S.wait("tensor", "VE")
    S.wait("gpsimd", "VE")
    S.dma("gpsimd", b.XFM[64:128, 1024:1536], XAT[0:64, 512:1024], "WD")
    S.wait("tensor", "WD")
    XAT2 = b.XFM[64:128, 1024:1536]

    for fc in range(NC_):
        for po in (0, 64):
            h = fc * 2 + po // 64
            at = XAT[0:64, fc * 64:fc * 64 + 64] if po == 0 else \
                 XAT2[:, fc * 64:fc * 64 + 64]
            for ts in range(4):
                S.mm_group([(b.PA[po:po + 64, ts * 512:(ts + 1) * 512],
                             at,
                             b.BUF_B[po:po + 64, fc * SEQ + ts * 512: fc * SEQ + (ts + 1) * 512])],
                           ("PE", 1))
        S.wait("vector", "PE")
        S.call("vector", "tensor_copy", out=b.BUF_A[:, fc * SEQ:(fc + 1) * SEQ],
               in_=b.PA[:], inc=("VE", 1))
        S.wait("tensor", "VE")
    for tt in range(NT):
        for h2 in range(2):
            S.mm_group([(b.PB[:, h2 * 512:(h2 + 1) * 512],
                         b.BUF_A[:, cin * SEQ + tt * 128: cin * SEQ + (tt + 1) * 128],
                         slot1[:, cin * 1024 + h2 * 512: cin * 1024 + (h2 + 1) * 512])
                        for cin in range(NC_)], ("PE", 1))
        S.wait("vector", "PE")
        S.call("vector", "tensor_tensor", out=b.XRES[:, tt * D:(tt + 1) * D],
               in0=b.XRES[:, tt * D:(tt + 1) * D], in1=b.PB[:], op=ALU.add,
               inc=("VE", 1))
        S.wait("tensor", "VE")




def emit_moe1_gate(S, b, dram):
    gwf = b.BUF_B[:, 4096:4608].bitcast(F32)
    S.wait("sync", "PE")
    S.wait("sync", "VE")
    S.dma("sync", gwf.rearrange("p (c e) -> p c e", c=NC_),
          dram["m1_gwT"].rearrange("(c p) e -> p c e", p=128), "XD")
    gb_bf = b.SBF[0:1, SB_MISC:SB_MISC + 32]
    S.wait("gpsimd", "PE")
    S.dma("gpsimd", gb_bf, dram["m1_gbias"][None, :], "WD")
    S.wait("tensor", "WD")
    S.wait("tensor", "XD")
    S.wait("tensor", "VE")
    S.mm_group([(b.PC[:, 0:32], b.SBF[0:1, SB_ONES:SB_ONES + 128], gb_bf)],
               ("PE", 1))
    S.wait("vector", "PE")
    gbr = b.SMALL[:, SM_X:SM_X + 32]
    S.call("vector", "tensor_copy", out=gbr, in_=b.PC[:, 0:32], inc=("VE", 1))
    S.wait("tensor", "VE")
    S.mm_group([(b.PC[0:1, 32:64], b.CONST[:, C_ONE:C_ONE + 1],
                 gwf[:, c * 32:(c + 1) * 32]) for c in range(NC_)], ("PE", 1))
    S.wait("vector", "PE")
    gsrow = b.SMALL[0:1, SM_X + 32:SM_X + 64]
    S.call("vector", "tensor_copy", out=gsrow, in_=b.PC[0:1, 32:64], inc=("VE", 1))
    S.wait("tensor", "VE")
    S.mm_group([(b.PC[:, 32:64], b.CONST[0:1, C_ONER:C_ONER + 128], gsrow)],
               ("PE", 1))
    S.wait("vector", "PE")
    gsr = b.SMALL[:, 736:768]
    S.call("vector", "tensor_copy", out=gsr, in_=b.PC[:, 32:64])
    negR = b.SMALL[:, SM_W:SM_W + NT]
    S.call("vector", "tensor_scalar", out=negR, in0=b.SMALL[:, SM_R:SM_R + NT],
           scalar1=-1.0, scalar2=None, op0=ALU.mult, inc=("VE", 1))
    S.wait("tensor", "VE")
    xT = b.BUF_B[:, 0:2048].bitcast(F32)
    SC = b.SMALL[:, SM_GATE:SM_GATE + 32]
    SCC = b.SMALL[:, SM_GATE + 32:SM_GATE + 64]
    T32 = b.SMALL[:, SM_S:SM_S + 32]
    MO = b.BUF_B[:, 2560:2624].bitcast(F32)
    m1c = b.SMALL[:, SM_T:SM_T + 1]
    m2c = b.SMALL[:, SM_T + 1:SM_T + 2]
    gsum = b.SMALL[:, SM_T + 2:SM_T + 6]
    gthr = b.SMALL[:, SM_T + 6:SM_T + 7]
    gmask = b.SMALL[:, SM_T + 8:SM_T + 12]
    thr8 = b.SMALL[:, SM_T + 12:SM_T + 13]
    den = b.SMALL[:, SM_T + 13:SM_T + 14]
    SC16 = b.BUF_B[:, 6144:7168].bitcast(F32)
    ac_hist = []
    for tt in range(NT):
        PTf = b.PT[:].bitcast(F32)
        lp = b.PC[:, 0:32] if tt % 2 == 0 else b.PB[:, 0:32]
        if len(ac_hist) >= 2:
            S.wait("tensor", "AC", val=ac_hist[-2])
        for half in range(2):
            S.wait("tensor", "VE")
            S.transposes([(PTf[:, c * 128:(c + 1) * 128],
                           b.XRES[:, tt * D + (half * 4 + c) * 128:
                                  tt * D + (half * 4 + c + 1) * 128])
                          for c in range(4)], b.CONST[:, C_IDF32:C_IDF32 + 128],
                         ("PE", 1))
            S.wait("vector", "PE")
            S.call("vector", "tensor_copy",
                   out=xT[:, half * 512:(half + 1) * 512], in_=PTf[:, 0:512],
                   inc=("VE", 1))
        S.wait("tensor", "VE")
        grp = [(lp, xT[:, cin * 128:(cin + 1) * 128],
                gwf[:, cin * 32:(cin + 1) * 32]) for cin in range(NC_)]
        S.mm_group(grp, ("PE", 1))
        S.wait("vector", "PE")
        S.chained("vector", "VE", "scalar_tensor_tensor", out=SCC, in0=gsr,
                  scalar=b.SMALL[:, SM_M + tt:SM_M + tt + 1], in1=lp,
                  op0=ALU.mult, op1=ALU.subtract)
        S.chained("vector", "VE", "tensor_scalar", out=SCC, in0=SCC,
                  scalar1=negR[:, tt:tt + 1], scalar2=None, op0=ALU.mult)
        S.wait("scalar", "VE")
        S.call("scalar", "activation", out=SC16[:, tt * 32:(tt + 1) * 32],
               in_=SCC, func=AF.Sigmoid, inc=("AC", 1))
        ac_hist.append(S.cnt["AC"])
    for tt in range(NT):
        SC = SC16[:, tt * 32:(tt + 1) * 32]
        S.wait("vector", "AC", val=ac_hist[tt])
        S.chained("vector", "VE", "tensor_tensor", out=SCC, in0=SC, in1=gbr,
                  op=ALU.add)
        for g in range(4):
            sl = SCC[:, g * 8:(g + 1) * 8]
            S.chained("vector", "VE", "reduce_max", out=m1c, in_=sl, axis=AX.X)
            S.chained("vector", "VE", "tensor_scalar", out=T32[:, 0:8], in0=sl,
                      scalar1=m1c, scalar2=None, op0=ALU.is_equal)
            S.chained("vector", "VE", "scalar_tensor_tensor", out=T32[:, 0:8],
                      in0=T32[:, 0:8], scalar=-1e30, in1=sl,
                      op0=ALU.mult, op1=ALU.add)
            S.chained("vector", "VE", "reduce_max", out=m2c, in_=T32[:, 0:8], axis=AX.X)
            S.chained("vector", "VE", "tensor_tensor", out=gsum[:, g:g + 1],
                      in0=m1c, in1=m2c, op=ALU.add)
        S.chained("vector", "VE", "reduce_max", out=m1c, in_=gsum, axis=AX.X)
        S.chained("vector", "VE", "tensor_scalar", out=T32[:, 0:4], in0=gsum,
                  scalar1=m1c, scalar2=None, op0=ALU.is_equal)
        S.chained("vector", "VE", "scalar_tensor_tensor", out=T32[:, 0:4],
                  in0=T32[:, 0:4], scalar=-1e30, in1=gsum,
                  op0=ALU.mult, op1=ALU.add)
        S.chained("vector", "VE", "reduce_max", out=gthr, in_=T32[:, 0:4], axis=AX.X)
        for g in range(4):
            S.chained("vector", "VE", "tensor_scalar", out=gmask[:, g:g + 1],
                      in0=gsum[:, g:g + 1], scalar1=gthr, scalar2=None,
                      op0=ALU.is_ge)
            S.chained("vector", "VE", "tensor_scalar", out=MO[:, g * 8:(g + 1) * 8],
                      in0=SCC[:, g * 8:(g + 1) * 8], scalar1=gmask[:, g:g + 1],
                      scalar2=None, op0=ALU.mult)
        S.chained("vector", "VE", "tensor_copy", out=T32, in_=MO)
        for r in range(8):
            S.chained("vector", "VE", "reduce_max", out=thr8, in_=T32, axis=AX.X)
            if r < 7:
                S.chained("vector", "VE", "tensor_scalar", out=SCC, in0=T32,
                          scalar1=thr8, scalar2=None, op0=ALU.is_equal)
                S.chained("vector", "VE", "scalar_tensor_tensor", out=T32,
                          in0=SCC, scalar=-1e30, in1=T32,
                          op0=ALU.mult, op1=ALU.add)
        S.chained("vector", "VE", "tensor_scalar", out=T32, in0=MO,
                  scalar1=thr8, scalar2=None, op0=ALU.is_ge)
        S.chained("vector", "VE", "tensor_tensor", out=T32, in0=T32, in1=SC,
                  op=ALU.mult)
        S.chained("vector", "VE", "reduce_sum", out=den, in_=T32, axis=AX.X)
        S.chained("vector", "VE", "tensor_scalar", out=den, in0=den,
                  scalar1=1e-20, scalar2=None, op0=ALU.add)
        S.chained("vector", "VE", "reciprocal", out=den, in_=den)
        S.chained("vector", "VE", "tensor_scalar",
                  out=b.SMALL[:, SM_WFIN + tt * 32:SM_WFIN + (tt + 1) * 32],
                  in0=T32, scalar1=den, scalar2=2.5, op0=ALU.mult, op1=ALU.mult)


def emit_moe1_shared(S, b, dram):
    slot0 = b.WS[:, 0:8192]
    slot1 = b.WS[:, 8192:16384]
    gs = b.BUF_A[:, 0:2048]
    PTf = b.PT[:].bitcast(F32)

    def load(nm, slot):
        S.wait("gpsimd", "PE")
        S.wait("gpsimd", "VE")
        S.dma("gpsimd", slot.rearrange("p (c f) -> p c f", c=NC_),
              dram["m1_" + nm].rearrange("(c p) f -> p c f", p=128), "WD")
        S.wait("tensor", "WD")

    load("sg", slot0)
    load("su", slot1)
    for sl8 in range(8):
        t0 = sl8 * 256
        for ic in range(NC_):
            S.mm_group([(b.PA[:, ic * 256:(ic + 1) * 256],
                         slot0[:, cin * 1024 + ic * 128: cin * 1024 + (ic + 1) * 128],
                         b.XFM[:, cin * SEQ + t0: cin * SEQ + t0 + 256])
                        for cin in range(NC_)], ("PE", 1))
        for ic in range(NC_):
            upsum = (b.PB[:, ic * 256:(ic + 1) * 256] if ic < 4 else
                     (b.PC[:, (ic - 4) * 256:(ic - 3) * 256] if ic < 6 else
                      PTf[:, (ic - 6) * 256:(ic - 5) * 256]))
            S.mm_group([(upsum,
                         slot1[:, cin * 1024 + ic * 128: cin * 1024 + (ic + 1) * 128],
                         b.XFM[:, cin * SEQ + t0: cin * SEQ + t0 + 256])
                        for cin in range(NC_)], ("PE", 1))
        S.wait("scalar", "PE")
        S.call("scalar", "activation", out=gs, in_=b.PA[:], func=AF.Silu,
               inc=("AC", 1))
        S.wait("vector", "AC")
        S.wait("vector", "PE")
        for ic in range(NC_):
            upsum = (b.PB[:, ic * 256:(ic + 1) * 256] if ic < 4 else
                     (b.PC[:, (ic - 4) * 256:(ic - 3) * 256] if ic < 6 else
                      PTf[:, (ic - 6) * 256:(ic - 5) * 256]))
            S.call("vector", "tensor_tensor",
                   out=b.BUF_B[:, ic * SEQ + t0: ic * SEQ + t0 + 256],
                   in0=gs[:, ic * 256:(ic + 1) * 256], in1=upsum, op=ALU.mult,
                   inc=(("VE", 1) if ic == NC_ - 1 else None))
        S.wait("tensor", "VE")
    load("sd", slot0)
    S.wait("tensor", "AC")
    S.wait("tensor", "VE")
    for tt in range(NT):
        for hf in range(2):
            S.mm_group([(b.PA[:, hf * 512:(hf + 1) * 512],
                         b.BUF_B[:, ic * SEQ + tt * 128: ic * SEQ + (tt + 1) * 128],
                         slot0[:, ic * 1024 + hf * 512: ic * 1024 + (hf + 1) * 512])
                        for ic in range(NC_)], ("PE", 1))
        S.wait("vector", "PE")
        S.call("vector", "tensor_tensor", out=b.XRES[:, tt * D:(tt + 1) * D],
               in0=b.XRES[:, tt * D:(tt + 1) * D], in1=b.PA[:, 0:1024],
               op=ALU.add, inc=("VE", 1))
        S.wait("tensor", "VE")


class Ctr:

    _cache = {}

    def __init__(self, nc, eng, name, init):
        key = (id(nc), eng.engine, name)
        if key in Ctr._cache:
            self.reg = Ctr._cache[key]
        else:
            regs = nc.alloc_registers(name, engines=[eng.engine])
            self.reg = list(regs)[0]
            Ctr._cache[key] = self.reg
        self.eng = eng
        nc.reg_mov(self.reg, init)

    def bump_wait(self, sem, delta):
        if delta:
            self.eng.scalar_reg_alu(ALU.add, self.reg, delta)
        self.eng.wait_ge(sem, self.reg)


def emit_moe1_experts(S, b, dram):
    wg = [b.WS[:, 0:4096], b.WS[:, 8192:8192 + 4096]]
    wu = [b.WS[:, 4096:8192], b.WS[:, 8192 + 4096:16384]]
    wd = [b.BUF_A[:, 0:4096], b.BUF_A[:, 4096:8192]]
    afm = [b.BUF_B[:, 0:2048], b.BUF_B[:, 2048:4096]]
    gsx = b.BUF_B[:, 4096:6144]
    PTf = b.PT[:].bitcast(F32)
    sems = S.sems
    dm = {"g": dram["m1_eg"], "u": dram["m1_eu"], "d": dram["m1_ed"]}
    NSLAB = 4
    wfv = b.SMALL[:].rearrange("p (t e) -> p t e", e=32)
    WOFF = SM_WFIN // 32
    wcur = [b.SMALL[:, SM_X + 32:SM_X + 48], b.SMALL[:, SM_X + 48:SM_X + 64]]

    wd0 = S.cnt["WD"]
    S.wait("gpsimd", "PE")
    S.wait("gpsimd", "VE")
    for p in range(2):
        S.dma("gpsimd", wg[p].rearrange("p (c i) -> p c i", c=NC_),
              dm["g"][p].rearrange("(c p) i -> p c i", p=128), "WD")
        S.dma("gpsimd", wu[p].rearrange("p (c i) -> p c i", c=NC_),
              dm["u"][p].rearrange("(c p) i -> p c i", p=128), "WD")
        S.dma("gpsimd", wd[p].rearrange("p (c h) -> p c h", c=4),
              dm["d"][p].rearrange("(c p) h -> p c h", p=128), "WD")

    pe0 = S.cnt["PE"]
    ve0 = S.cnt["VE"]
    ac0 = S.cnt["AC"]
    fz0 = S.cnt["FZ"]
    NSLAB = 8
    gsx2 = [b.BUF_B[:, 4096:5120], b.BUF_B[:, 5120:6144]]
    afm2 = [b.BUF_B[:, 0:1024], b.BUF_B[:, 1024:2048]]

    def g_psum(h):
        return b.PA[:, h * 1024:(h + 1) * 1024]

    def u_piece(h, ic):
        if h == 0:
            return b.PB[:, ic * 256:(ic + 1) * 256]
        return (b.PC[:, ic * 256:(ic + 1) * 256] if ic < 2
                else PTf[:, (ic - 2) * 256:(ic - 1) * 256])

    def d_out(h, tt_local, half):
        if tt_local == 0:
            return b.PA[:, h * 1024 + half * 512: h * 1024 + (half + 1) * 512]
        if h == 0:
            return b.PB[:, half * 512:(half + 1) * 512]
        return b.PC[:, 0:512] if half == 0 else PTf[:, 0:512]

    def GU(k):
        return 1 if k == 0 else (2 if k == 1 else 3 * k - 1)

    def D0(k):
        return 3 * k + 3 if k < NSLAB - 1 else 3 * NSLAB - 1
    def D1(k):
        return 3 * k + 4 if k < NSLAB - 1 else 3 * NSLAB

    S.begin_loop(16)
    for p in range(2):
        S.wait("gpsimd", "PE", val=pe0 + 24 * (p + 1))
        for nm, slot, c in (("g", wg, NC_), ("u", wu, NC_), ("d", wd, 4)):
            def fn(eng, nm=nm, slot=slot, p=p, c=c):
                i = S.loop_vars["gpsimd"][-1]
                e = i * 2 + 2 + p
                eng.dma_start(out=slot[p].rearrange("p (c i) -> p c i", c=c),
                              in_=dm[nm][e].rearrange("(c p) i -> p c i", p=128)
                              ).then_inc(sems["WD"], 16)
            S.raw("gpsimd", fn)
            S.inc("WD", 16)
    for p in range(2):
        e_first = None
        def gfn(eng, p=p):
            i = S.loop_vars["vector"][-1]
            e = i * 2 + p
            eng.tensor_copy(out=wcur[p], in_=wfv[:, WOFF:WOFF + 16, e]
                            ).then_inc(sems["FZ"], 1)
        S.raw("vector", gfn)
        S.inc("FZ", 1)
        S.wait("vector", "FZ", val=fz0 + p + 1)
        for k in range(NSLAB):
            n = p * NSLAB + k
            h = k % 2
            S.wait("vector", "AC", val=ac0 + n + 1)
            S.wait("vector", "PE", val=pe0 + 24 * p + GU(k))
            if h == 0:
                S.call("vector", "tensor_tensor", out=afm2[0],
                       in0=gsx2[0], in1=b.PB[:, 0:1024], op=ALU.mult,
                       inc=("VE", 1))
            else:
                S.call("vector", "tensor_tensor", out=afm2[1][:, 0:512],
                       in0=gsx2[1][:, 0:512], in1=b.PC[:, 0:512], op=ALU.mult)
                S.call("vector", "tensor_tensor", out=afm2[1][:, 512:1024],
                       in0=gsx2[1][:, 512:1024], in1=PTf[:, 0:512], op=ALU.mult,
                       inc=("VE", 1))
            for tl in range(2):
                tile = 2 * k + tl
                wc = wcur[p][:, tile:tile + 1]
                xr = b.XRES[:, tile * D:(tile + 1) * D]
                S.wait("vector", "PE",
                       val=pe0 + 24 * p + (D0(k) if tl == 0 else D1(k)))
                if tl == 0:
                    S.call("vector", "scalar_tensor_tensor", out=xr,
                           in0=g_psum(h), scalar=wc, in1=xr,
                           op0=ALU.mult, op1=ALU.add, inc=("VE", 1))
                elif h == 0:
                    S.call("vector", "scalar_tensor_tensor", out=xr,
                           in0=b.PB[:, 0:1024], scalar=wc, in1=xr,
                           op0=ALU.mult, op1=ALU.add, inc=("VE", 1))
                else:
                    S.call("vector", "scalar_tensor_tensor", out=xr[:, 0:512],
                           in0=b.PC[:, 0:512], scalar=wc, in1=xr[:, 0:512],
                           op0=ALU.mult, op1=ALU.add)
                    S.call("vector", "scalar_tensor_tensor", out=xr[:, 512:1024],
                           in0=PTf[:, 0:512], scalar=wc, in1=xr[:, 512:1024],
                           op0=ALU.mult, op1=ALU.add, inc=("VE", 1))
    for p in range(2):
        S.wait("tensor", "WD", val=wd0 + 48 * (p + 1))

        def emit_gu(k):
            t0 = k * 256
            n = p * NSLAB + k
            h = k % 2
            if n == 0:
                S.wait("tensor", "VE", val=ve0)
            if n >= 2:
                S.wait("tensor", "VE", val=ve0 + 3 * (n - 2) + 3)
            groups = []
            for ic in range(4):
                groups.append([(g_psum(h)[:, ic * 256:(ic + 1) * 256],
                                wg[p][:, cin * 512 + ic * 128:
                                      cin * 512 + (ic + 1) * 128],
                                b.XFM[:, cin * SEQ + t0: cin * SEQ + t0 + 256])
                               for cin in range(NC_)])
            for ic in range(4):
                groups.append([(u_piece(h, ic),
                                wu[p][:, cin * 512 + ic * 128:
                                      cin * 512 + (ic + 1) * 128],
                                b.XFM[:, cin * SEQ + t0: cin * SEQ + t0 + 256])
                               for cin in range(NC_)])
            S.mm_multi(groups, ("PE", 1))

        def emit_down(k):
            n = p * NSLAB + k
            h = k % 2
            S.wait("tensor", "VE", val=ve0 + 3 * n + 1)
            S.wait("tensor", "AC", val=ac0 + n + 1)
            for tl in range(2):
                groups = []
                for half in range(2):
                    groups.append([(d_out(h, tl, half),
                                    afm2[h][:, ic * 256 + tl * 128:
                                            ic * 256 + tl * 128 + 128],
                                    wd[p][:, ic * 1024 + half * 512:
                                          ic * 1024 + (half + 1) * 512])
                                   for ic in range(4)])
                S.mm_multi(groups, ("PE", 1))

        emit_gu(0)
        for k in range(1, NSLAB):
            emit_gu(k)
            emit_down(k - 1)
        emit_down(NSLAB - 1)
    for p in range(2):
        for k in range(NSLAB):
            n = p * NSLAB + k
            h = k % 2
            S.wait("scalar", "PE", val=pe0 + 24 * p + GU(k))
            if 3 * n - 5 > 0:
                S.wait("scalar", "VE", val=max(ve0 + 3 * n - 5, 1))
            S.call("scalar", "activation", out=gsx2[h], in_=g_psum(h),
                   func=AF.Silu, inc=("AC", 1))
    S.end_loop()




def _prep_consts():
    q = np.arange(128)
    return {
        "ones128": np.ones((128, 1), np.float32),
        "ident": np.eye(128, dtype=np.float32),
        "ones512r": np.ones((1, 512), np.float32),
        "tri_lo": np.where(q[:, None] >= q[None, :], 0.0, -1e30).astype(np.float32),
        "tri_hi": np.where(q[:, None] <= q[None, :], 0.0, -1e30).astype(np.float32),
    }


CONST_SHAPES = [("ones128", [128, 1]), ("ident", [128, 128]),
                ("ones512r", [1, 512]), ("tri_lo", [128, 128]),
                ("tri_hi", [128, 128])]


def _prep_ffn0(inp):
    sel = np.zeros((8, 512), np.float32)
    for cc in range(4):
        for p in range(128):
            sel[cc * 2 + p // 64, cc * 128 + p] = 1.0
    return {
        "f0_gw_flat": np.ascontiguousarray(
            inp["L0_ffn_gate_w"].reshape(1, 8192)),
        "f0_WdT": np.ascontiguousarray(inp["L0_ffn_down_w"].reshape(512, 1024).T),
        "f0_bd": np.ascontiguousarray(inp["L0_ffn_down_b"].reshape(512)),
        "f0_WuT": np.ascontiguousarray(
            inp["L0_ffn_up_w"].transpose(0, 2, 1).reshape(512, 1024)),
        "f0_bu": np.ascontiguousarray(inp["L0_ffn_up_b"]),
        "f0_SEL": sel,
    }


FFN0_SHAPES = [("f0_gw_flat", [1, 8192]), ("f0_WdT", [1024, 512]),
               ("f0_bd", [512]), ("f0_WuT", [512, 1024]),
               ("f0_bu", [8, 1024]), ("f0_SEL", [8, 512])]


def _prep_attn_b0(inp):
    W = np.asarray(inp["L0_attn_b_W_qkv"])
    bqkv = np.asarray(inp["L0_attn_b_b_qkv"])
    cols = np.arange(3072)
    h = cols // 192
    j = cols % 192
    qcols = (h * 64 + j)[j < 64]
    kcols = (h * 64 + (j - 64))[(j >= 64) & (j < 128)]
    vcols = (h * 64 + (j - 128))[j >= 128]
    qi = np.zeros(1024, np.int64); qi[qcols] = cols[j < 64]
    ki = np.zeros(1024, np.int64); ki[kcols] = cols[(j >= 64) & (j < 128)]
    vi = np.zeros(1024, np.int64); vi[vcols] = cols[j >= 128]
    return {
        "ab0_Wq": np.ascontiguousarray(W[:, qi]),
        "ab0_Wk": np.ascontiguousarray(W[:, ki]),
        "ab0_Wv": np.ascontiguousarray(W[:, vi]),
        "ab0_Wo": np.ascontiguousarray(inp["L0_attn_b_W_out"]),
        "ab0_bq": np.ascontiguousarray(bqkv[qi]),
        "ab0_bk": np.ascontiguousarray(bqkv[ki]),
        "ab0_bv": np.ascontiguousarray(bqkv[vi]),
        "ab0_bo": np.ascontiguousarray(inp["L0_attn_b_b_out"]),
    }


ATTN_B0_SHAPES = [("ab0_" + n, [1024, 1024]) for n in ("Wq", "Wk", "Wv", "Wo")] + \
                 [("ab0_" + n, [1024]) for n in ("bq", "bk", "bv", "bo")]




def build_program(sublayers):
    nc = bass.Bass("TRN2", target_bir_lowering=False)
    dram = {}
    dram["x_in"] = nc.dram_tensor("x_in", [SPC, SEQ, D], F32,
                                  kind="ExternalInput").ap()
    dram["x_out"] = nc.dram_tensor("x_out", [SPC, SEQ, D], F32,
                                   kind="ExternalOutput").ap()
    shapes = list(CONST_SHAPES)
    if "ffn0" in sublayers:
        shapes += FFN0_SHAPES
    if "attn_b0" in sublayers:
        shapes += ATTN_B0_SHAPES
    if "attn_a0" in sublayers:
        shapes += ATTN_A0_SHAPES
    if "attn_a1" in sublayers:
        shapes += ATTN_A1_SHAPES
    if "attn_b1" in sublayers:
        shapes += ATTN_B1_SHAPES
    if "moe1" in sublayers:
        shapes += MOE1_SHAPES
    for name, shape in shapes:
        dram[name] = nc.dram_tensor(name, shape, F32, kind="ExternalInput").ap()

    with ExitStack() as ctx:
        b = alloc_bufs(nc, ctx)
        sems = {}
        for name in ("WD", "XD", "PE", "VE", "AC", "FZ"):
            sems[name] = ctx.enter_context(nc.semaphore(name))
        S = Sched(sems, nc)
        emit_consts(S, b, dram)
        for e in ("vector", "tensor", "scalar"):
            S.wait(e, "WD")
            S.wait(e, "XD")
        S.begin_loop(SPC)
        if True:
            S.wait("sync", "XD")
            S.wait("sync", "VE")

            def ldx(eng):
                s = S.loop_vars["sync"][-1]
                eng.dma_start(out=b.XRES[:].rearrange("p (i d) -> p i d", i=NT),
                              in_=dram["x_in"][s].rearrange("(i p) d -> p i d", p=128)
                              ).then_inc(sems["XD"], 16)
            S.raw("sync", ldx)
            S.inc("XD", 16)
            S.wait("vector", "XD")
            S.wait("scalar", "XD")
            S.wait("gpsimd", "XD")
            S.wait("tensor", "XD")
            for kind in sublayers:
                if kind == "ffn0":
                    emit_ln0(S, b)
                    emit_ffn0(S, b, dram)
                elif kind == "attn_a0":
                    S.dma("sync", b.CONST[:, C_G:C_G + 8], dram["aa0_g8"], "XD")
                    S.wait("vector", "XD")
                    emit_ln0(S, b, scale_cols=b.SMALL[:, SMX_S:SMX_S + NT],
                             post_stats=lambda: emit_aa0_cols(S, b))
                    emit_attn_a0(S, b, dram)
                elif kind == "attn_b0":
                    emit_ln0(S, b)
                    emit_mha(S, b, dram, "ab0")
                elif kind == "attn_a1":
                    S.dma("sync", b.CONST[:, C_G:C_G + 8], dram["aa1_g8"], "XD")
                    S.dma("gpsimd", b.CBF[:, CB_B8:CB_B8 + 8], dram["aa1_b8"], "WD")
                    S.wait("vector", "XD")
                    S.wait("tensor", "XD")
                    S.wait("tensor", "WD")
                    emit_ln0(S, b, scale_cols=b.SMALL[:, SMX_S:SMX_S + NT],
                             post_stats=lambda: emit_ln2cols(S, b, want_resid=True))
                    emit_mha2(S, b, dram, "aa1", rms=False, band=True,
                              inner_beta=True, out_bias=False, inner_residual=True)
                elif kind == "attn_b1":
                    S.dma("sync", b.CONST[:, C_G:C_G + 8], dram["ab1_g8"], "XD")
                    S.dma("gpsimd", b.CBF[:, CB_B8:CB_B8 + 8], dram["ab1_b8"], "WD")
                    S.dma("sync", b.CONST[:, C_QG:C_QG + 1], dram["ab1_qg"], "XD")
                    S.dma("sync", b.CONST[:, C_KG:C_KG + 1], dram["ab1_kg"], "XD")
                    S.wait("vector", "XD")
                    S.wait("tensor", "XD")
                    S.wait("tensor", "WD")
                    emit_ln0(S, b, scale_cols=b.SMALL[:, SMX_S:SMX_S + NT],
                             post_stats=lambda: emit_ln2cols(S, b, want_resid=False))
                    emit_mha2(S, b, dram, "ab1", rms=True, band=False,
                              inner_beta=True, out_bias=True, inner_residual=False)
                elif kind == "moe1":
                    emit_ln0(S, b)
                    emit_moe1_gate(S, b, dram)
                    emit_moe1_shared(S, b, dram)
                    emit_moe1_experts(S, b, dram)
                else:
                    raise ValueError(kind)
            S.wait("sync", "VE")

            def stx(eng):
                s = S.loop_vars["sync"][-1]
                eng.dma_start(out=dram["x_out"][s].rearrange("(i p) d -> p i d", p=128),
                              in_=b.XRES[:].rearrange("p (i d) -> p i d", i=NT)
                              ).then_inc(sems["XD"], 16)
            S.raw("sync", stx)
            S.inc("XD", 16)
        S.end_loop()
        S.wait("sync", "XD")
        emit_block(nc, S)
    return nc


def _prep_attn_a1(inp):
    T = np.ascontiguousarray(np.asarray(inp["L1_attn_a_w_qkv"]).T)
    return {
        "aa1_Wq": np.ascontiguousarray(T[:, 0:1024]),
        "aa1_Wk": np.ascontiguousarray(T[:, 1024:2048]),
        "aa1_Wv": np.ascontiguousarray(T[:, 2048:3072]),
        "aa1_Wo": np.ascontiguousarray(np.asarray(inp["L1_attn_a_w_out"]).T),
        "aa1_g8": np.ascontiguousarray(np.asarray(inp["L1_attn_a_gamma"]).reshape(8, 128).T),
        "aa1_b8": np.ascontiguousarray(np.asarray(inp["L1_attn_a_beta"]).reshape(8, 128).T),
    }


ATTN_A1_SHAPES = [("aa1_" + n, [1024, 1024]) for n in ("Wq", "Wk", "Wv", "Wo")] + \
                 [("aa1_g8", [128, 8]), ("aa1_b8", [128, 8])]


def _prep_attn_b1(inp):
    return {
        "ab1_Wq": np.ascontiguousarray(np.asarray(inp["L1_attn_b_Wq"]).T),
        "ab1_Wk": np.ascontiguousarray(np.asarray(inp["L1_attn_b_Wk"]).T),
        "ab1_Wv": np.ascontiguousarray(np.asarray(inp["L1_attn_b_Wv"]).T),
        "ab1_Wo": np.ascontiguousarray(np.asarray(inp["L1_attn_b_Wo"]).T),
        "ab1_bo": np.ascontiguousarray(np.asarray(inp["L1_attn_b_bo"])),
        "ab1_g8": np.ascontiguousarray(np.asarray(inp["L1_attn_b_ln_g"]).reshape(8, 128).T),
        "ab1_b8": np.ascontiguousarray(np.asarray(inp["L1_attn_b_ln_b"]).reshape(8, 128).T),
        "ab1_qg": np.ascontiguousarray(np.tile(np.asarray(inp["L1_attn_b_q_gamma"]), 2)[:, None]),
        "ab1_kg": np.ascontiguousarray(np.tile(np.asarray(inp["L1_attn_b_k_gamma"]), 2)[:, None]),
    }


ATTN_B1_SHAPES = [("ab1_" + n, [1024, 1024]) for n in ("Wq", "Wk", "Wv", "Wo")] + \
                 [("ab1_bo", [1024]), ("ab1_g8", [128, 8]), ("ab1_b8", [128, 8]),
                  ("ab1_qg", [128, 1]), ("ab1_kg", [128, 1])]


def _prep_attn_a0(inp):
    W = np.asarray(inp["L0_attn_a_w_qkv"])
    return {
        "aa0_Wq": np.ascontiguousarray(W[:, 0:1024]),
        "aa0_Wk": np.ascontiguousarray(W[:, 1024:2048]),
        "aa0_Wv": np.ascontiguousarray(W[:, 2048:3072]),
        "aa0_Wo": np.ascontiguousarray(np.asarray(inp["L0_attn_a_w_out"])),
        "aa0_g8": np.ascontiguousarray(np.asarray(inp["L0_attn_a_gamma"]).reshape(8, 128).T),
        "aa0_temp": np.ascontiguousarray(np.asarray(inp["L0_attn_a_temperature"]).reshape(1, 16)),
    }


ATTN_A0_SHAPES = [("aa0_" + n, [1024, 1024]) for n in ("Wq", "Wk", "Wv", "Wo")] + \
                 [("aa0_g8", [128, 8]), ("aa0_temp", [1, 16])]


def _pad34(a):
    return np.ascontiguousarray(np.concatenate([a, a[:2]], axis=0))


def _prep_moe1(inp):
    return {
        "m1_gwT": np.ascontiguousarray(np.asarray(inp["L1_ffn_gate_w"]).T),
        "m1_gbias": np.ascontiguousarray(np.asarray(inp["L1_ffn_gate_bias"])),
        "m1_eg": _pad34(np.asarray(inp["L1_ffn_expert_gate"]).transpose(0, 2, 1)),
        "m1_eu": _pad34(np.asarray(inp["L1_ffn_expert_up"]).transpose(0, 2, 1)),
        "m1_ed": _pad34(np.asarray(inp["L1_ffn_expert_down"]).transpose(0, 2, 1)),
        "m1_sg": np.ascontiguousarray(np.asarray(inp["L1_ffn_shared_gate"]).T),
        "m1_su": np.ascontiguousarray(np.asarray(inp["L1_ffn_shared_up"]).T),
        "m1_sd": np.ascontiguousarray(np.asarray(inp["L1_ffn_shared_down"]).T),
    }


MOE1_SHAPES = [("m1_gwT", [1024, 32]), ("m1_gbias", [32]),
               ("m1_eg", [34, 1024, 512]), ("m1_eu", [34, 1024, 512]),
               ("m1_ed", [34, 512, 1024]),
               ("m1_sg", [1024, 1024]), ("m1_su", [1024, 1024]),
               ("m1_sd", [1024, 1024])]


PREP = {"ffn0": _prep_ffn0, "attn_b0": _prep_attn_b0,
        "attn_a1": _prep_attn_a1, "attn_b1": _prep_attn_b1,
        "attn_a0": _prep_attn_a0, "moe1": _prep_moe1}
SUB_SHAPES = {"ffn0": FFN0_SHAPES}


CHAIN = ["attn_a0", "ffn0", "attn_b0", "ffn0", "attn_a1", "moe1", "attn_b1", "moe1"]

_PROG_CACHE = {}


def _get_program(kinds):
    key = tuple(kinds)
    if key not in _PROG_CACHE:
        _PROG_CACHE[key] = build_program(list(kinds))
    return _PROG_CACHE[key]


def kernel(**inputs):
    inputs = {k: np.asarray(v) for k, v in inputs.items()}
    x = np.ascontiguousarray(inputs["L0_attn_a_x"], dtype=np.float32)
    consts = _prep_consts()
    weights = {}
    for kind in set(CHAIN):
        weights[kind] = PREP[kind](inputs)

    nc = _get_program(CHAIN)
    in_maps = []
    for c in range(N_CORES):
        m = {"x_in": np.ascontiguousarray(x[c * SPC:(c + 1) * SPC])}
        m.update(consts)
        for kind in set(CHAIN):
            m.update(weights[kind])
        in_maps.append(m)
    res = run_bass_kernel_spmd(nc, in_maps, core_ids=list(range(N_CORES)))
    return np.concatenate([r["x_out"] for r in res.results], axis=0)


# version 11
# speedup vs baseline: 1.1638x; 1.1638x over previous
from contextlib import ExitStack

import numpy as np
import concourse.bass as bass
import concourse.mybir as mybir
from concourse.bass_utils import run_bass_kernel_spmd

F32 = mybir.dt.float32
BF16 = mybir.dt.bfloat16
AF = mybir.ActivationFunctionType
ALU = mybir.AluOpType
AX = mybir.AxisListType

N_CORES = 8
SEQ = 2048
D = 1024
SPC = 4
NT = SEQ // 128
NC_ = D // 128
LN_EPS = 1e-5

ENGINES = ("sync", "vector", "tensor", "scalar", "gpsimd")


class Sched:

    def __init__(self, sems, nc=None):
        self.ops = {e: [] for e in ENGINES}
        self.sems = sems
        self.nc = nc
        self.cnt = {name: 0 for name in sems}
        self.frames = []
        self.last_wait = {}
        self.ctrs = {}
        self.loop_vars = {e: [] for e in ENGINES}

    def raw(self, e, fn):
        self.ops[e].append(fn)

    def inc(self, sem, n):
        self.cnt[sem] += n
        return self.cnt[sem]

    def call(self, e, meth, *args, inc=None, **kw):
        sems = self.sems
        if inc is not None:
            self.inc(*inc)

        def fn(eng):
            ins = getattr(eng, meth)(*args, **kw)
            if inc is not None:
                ins.then_inc(sems[inc[0]], inc[1])
        self.raw(e, fn)

    def _ctr(self, e, sem):
        key = (e, sem)
        if key not in self.ctrs:
            ctr = Ctr.__new__(Ctr)
            ctr.eng = None
            ctr.reg = None
            ctr.name = f"cw_{e}_{sem}"
            self.ctrs[key] = ctr
            init_val = self.frames[0]["entry_cnt"][sem]
            self.last_wait[key] = init_val
            nc = self.nc

            def init_fn(eng, ctr=ctr, iv=init_val):
                regs = nc.alloc_registers(ctr.name, engines=[eng.engine])
                ctr.reg = list(regs)[0]
                ctr.eng = eng
                nc.reg_mov(ctr.reg, iv)
            self.frames[0]["init_ops"][e].append(init_fn)
            for fr in self.frames:
                fr["entry_last"].setdefault(key, self.last_wait[key])
        return self.ctrs[key]

    def wait(self, e, sem, val=None):
        v = self.cnt[sem] if val is None else val
        if v <= 0:
            return
        h = self.sems[sem]
        if not self.frames:
            self.raw(e, lambda eng: eng.wait_ge(h, v))
            return
        ctr = self._ctr(e, sem)
        key = (e, sem)
        delta = v - self.last_wait[key]
        if delta < 0:
            return
        self.last_wait[key] = v
        self.frames[-1]["used"].add(key)

        def fn(eng, ctr=ctr, d=delta):
            if d:
                eng.scalar_reg_alu(ALU.add, ctr.reg, d)
            eng.wait_ge(h, ctr.reg)
        self.raw(e, fn)

    def begin_loop(self, trip):
        fr = {"trip": trip, "entry_cnt": dict(self.cnt),
              "entry_last": {k: self.last_wait[k] for k in self.last_wait},
              "used": set(), "init_ops": {e: [] for e in ENGINES},
              "id": len(self.frames)}
        self.frames.append(fr)
        marker = ("LOOP_START", fr)
        for e in ENGINES:
            self.ops[e].append(marker)
        return fr

    def end_loop(self):
        fr = self.frames.pop()
        trip = fr["trip"]
        totals = {s: self.cnt[s] - fr["entry_cnt"][s] for s in self.cnt}
        for key in fr["used"]:
            e, sem = key
            ctr = self.ctrs[key]
            adv = self.last_wait[key] - fr["entry_last"].get(key, self.last_wait[key])
            drift = totals[sem] - adv

            def fn(eng, ctr=ctr, d=drift):
                if d:
                    eng.scalar_reg_alu(ALU.add, ctr.reg, d)
            self.ops[e].append(fn)
            self.last_wait[key] = fr["entry_last"].get(key, self.last_wait[key]) \
                + trip * totals[sem]
        for s in self.cnt:
            self.cnt[s] = fr["entry_cnt"][s] + trip * totals[s]
        marker = ("LOOP_END", fr)
        for e in ENGINES:
            self.ops[e].append(marker)
        if self.frames:
            self.frames[-1]["used"] |= fr["used"]
            for key in fr["used"]:
                self.frames[-1]["entry_last"].setdefault(key, None)

    def dma(self, e, out, in_, sem):
        self.call(e, "dma_start", inc=(sem, 16), out=out, in_=in_)

    def chained(self, e, sem, meth, *args, **kw):
        self.call(e, meth, *args, inc=(sem, 1), **kw)
        self.wait(e, sem)

    def mm_group(self, mms, inc):
        sems = self.sems
        self.inc(*inc)
        n = len(mms)

        def fn(eng):
            ins = None
            for j, (o, l, r) in enumerate(mms):
                ins = eng.matmul(out=o, lhsT=l, rhs=r,
                                 start=(j == 0), stop=(j == n - 1))
            ins.then_inc(sems[inc[0]], inc[1])
        self.raw("tensor", fn)

    def mm_multi(self, groups, inc):
        sems = self.sems
        self.inc(*inc)

        def fn(eng):
            ins = None
            for grp in groups:
                n = len(grp)
                for j, (o, l, r) in enumerate(grp):
                    ins = eng.matmul(out=o, lhsT=l, rhs=r,
                                     start=(j == 0), stop=(j == n - 1))
            ins.then_inc(sems[inc[0]], inc[1])
        self.raw("tensor", fn)

    def transposes(self, trs, identity, inc):
        sems = self.sems
        self.inc(*inc)

        def fn(eng):
            ins = None
            for (o, i_) in trs:
                ins = eng.transpose(out=o, in_=i_, identity=identity)
            ins.then_inc(sems[inc[0]], inc[1])
        self.raw("tensor", fn)


def _run_ops(S, e, eng, ops, idx):
    from contextlib import ExitStack
    while idx < len(ops):
        item = ops[idx]
        if isinstance(item, tuple) and item[0] == "LOOP_START":
            fr = item[1]
            for f in fr["init_ops"][e]:
                f(eng)
            with eng.Fori(0, fr["trip"]) as v:
                S.loop_vars[e].append(v)
                idx = _run_ops(S, e, eng, ops, idx + 1)
                S.loop_vars[e].pop()
        elif isinstance(item, tuple) and item[0] == "LOOP_END":
            return idx
        else:
            item(eng)
        idx += 1
    return idx


def emit_block(nc, S):
    with nc.Block() as block:
        @block.sync
        def _(eng):
            _run_ops(S, "sync", eng, S.ops["sync"], 0)

        @block.vector
        def _(eng):
            _run_ops(S, "vector", eng, S.ops["vector"], 0)

        @block.tensor
        def _(eng):
            _run_ops(S, "tensor", eng, S.ops["tensor"], 0)

        @block.scalar
        def _(eng):
            _run_ops(S, "scalar", eng, S.ops["scalar"], 0)

        @block.gpsimd
        def _(eng):
            _run_ops(S, "gpsimd", eng, S.ops["gpsimd"], 0)


class Bufs:
    pass


def alloc_bufs(nc, ctx):
    b = Bufs()
    b.nc = nc
    ent = ctx.enter_context
    b.XRES = ent(nc.sbuf_tensor("XRES", [128, NT * D], F32))
    b.XFM = ent(nc.sbuf_tensor("XFM", [128, NC_ * SEQ], BF16))
    b.BUF_A = ent(nc.sbuf_tensor("BUF_A", [128, 16384], BF16))
    b.BUF_B = ent(nc.sbuf_tensor("BUF_B", [128, 16384], BF16))
    b.WS = ent(nc.sbuf_tensor("WS", [128, 16384], BF16))
    b.SMALL = ent(nc.sbuf_tensor("SMALL", [128, 768], F32))
    b.SBF = ent(nc.sbuf_tensor("SBF", [128, 4608], BF16))
    b.CONST = ent(nc.sbuf_tensor("CONST", [128, 576], F32))
    b.CBF = ent(nc.sbuf_tensor("CBF", [128, 160], BF16))
    b.PA = ent(nc.psum_tensor("PA", [128, 2048], F32))
    b.PB = ent(nc.psum_tensor("PB", [128, 1024], F32))
    b.PC = ent(nc.psum_tensor("PC", [128, 512], F32))
    b.PT = ent(nc.psum_tensor("PT", [128, 1024], BF16))
    return b


SM_S, SM_Q, SM_M, SM_R, SM_T, SM_W = 0, 16, 32, 48, 64, 80
SM_GATE = 96
SM_WFIN = 160
SM_X = 672

SB_ONES = 0
SB_BD = 512
SB_BUW = 1024
SB_MISC = 2048
SB_SEL = 2176
SB_BU8 = 2688
SB_JUNK = 3712
SB_BIAS1 = SB_BUW
SB_BIAS2 = SB_SEL

CBF_ONESC = 0
CBF_ID = 1
CB_B8 = 136
C_TRI_LO, C_TRI_HI = 0, 128
C_G = 256
C_QG = 272
C_KG = 273
C_ONE = 274
C_ONER = 280
C_IDF32 = 416
C_SH = 275
C_B2 = 264


def emit_consts(S, b, dram):
    S.dma("gpsimd", b.CBF[:, 0:1], dram["ones128"], "WD")
    S.dma("gpsimd", b.CBF[:, 1:129], dram["ident"], "WD")
    S.dma("gpsimd", b.SBF[0:1, 0:512], dram["ones512r"], "WD")
    S.dma("sync", b.CONST[:, 0:128], dram["tri_lo"], "XD")
    S.dma("sync", b.CONST[:, 128:256], dram["tri_hi"], "XD")
    S.dma("sync", b.CONST[:, C_ONE:C_ONE + 1], dram["ones128"], "XD")
    S.dma("sync", b.CONST[0:1, C_ONER:C_ONER + 128], dram["ones512r"][0:1, 0:128], "XD")
    S.dma("sync", b.CONST[:, C_IDF32:C_IDF32 + 128], dram["ident"], "XD")




def emit_ln0(S, b, out_tok=None, out_fm=True, scale_cols=None, post_stats=None):
    if out_tok is None:
        out_tok = b.BUF_A[:, 0:NT * D]
    S.call("vector", "memset", b.SMALL[:, SM_Q:SM_Q + NT], 0.0, inc=("VE", 1))
    S.wait("scalar", "VE")
    for i in range(NT):
        x = b.XRES[:, i * D:(i + 1) * D]
        S.call("vector", "reduce_sum", out=b.SMALL[:, SM_S + i:SM_S + i + 1],
               in_=x, axis=AX.X,
               inc=(("VE", 1) if i == NT - 1 else None))
        S.call("scalar", "activation", out=out_tok[:, i * D:(i + 1) * D],
               in_=x, func=AF.Square,
               accum_out=b.SMALL[:, SM_Q + i:SM_Q + i + 1],
               inc=(("AC", 1) if i == NT - 1 else None))
    S.wait("vector", "VE")
    sS = b.SMALL[:, SM_S:SM_S + NT]
    sQ = b.SMALL[:, SM_Q:SM_Q + NT]
    sM = b.SMALL[:, SM_M:SM_M + NT]
    sR = b.SMALL[:, SM_R:SM_R + NT]
    S.chained("vector", "VE", "tensor_scalar", out=sM, in0=sS, scalar1=1.0 / D,
              scalar2=None, op0=ALU.mult)
    S.wait("vector", "AC")
    S.chained("vector", "VE", "tensor_scalar", out=sQ, in0=sQ, scalar1=1.0 / D,
              scalar2=None, op0=ALU.mult)
    S.chained("vector", "VE", "scalar_tensor_tensor", out=sR, in0=sM, scalar=1.0,
              in1=sM, op0=ALU.mult, op1=ALU.mult)
    S.chained("vector", "VE", "tensor_tensor", out=sR, in0=sR, in1=sQ,
              op=ALU.subtract)
    S.chained("vector", "VE", "tensor_scalar", out=sR, in0=sR, scalar1=-1.0,
              scalar2=LN_EPS, op0=ALU.mult, op1=ALU.add)
    S.chained("vector", "VE", "tensor_copy", out=b.SMALL[:, SM_W:SM_W + NT], in_=sR)
    S.wait("scalar", "VE")
    S.chained("scalar", "AC", "activation", out=sR, in_=sR, func=AF.Ln,
              scale=1.0)
    S.call("scalar", "activation", out=sR, in_=sR, func=AF.Exp,
           scale=-0.5, inc=("AC", 1))
    S.wait("vector", "AC")
    if post_stats is not None:
        post_stats()
    for i in range(NT):
        x = b.XRES[:, i * D:(i + 1) * D]
        sc2 = sR[:, i:i + 1] if scale_cols is None else scale_cols[:, i:i + 1]
        S.call("vector", "tensor_scalar", out=out_tok[:, i * D:(i + 1) * D],
               in0=x, scalar1=sM[:, i:i + 1], scalar2=sc2,
               op0=ALU.subtract, op1=ALU.mult,
               inc=(("VE", 1) if i == NT - 1 else None))
    if out_fm:
        emit_tok2fm(S, b, out_tok, b.XFM)


def emit_tok2fm(S, b, src_tok, dst_fm):
    ident = b.CBF[:, 1:129]
    dst = dst_fm if not hasattr(dst_fm, "ap") else dst_fm
    for i in range(NT):
        S.wait("tensor", "VE")
        trs = [(b.PT[:, c * 128:(c + 1) * 128],
                src_tok[:, i * D + c * 128: i * D + (c + 1) * 128])
               for c in range(NC_)]
        S.transposes(trs, ident, ("PE", 1))
        S.wait("vector", "PE")
        S.call("vector", "tensor_copy",
               out=dst.rearrange("p (c t) -> p c t", c=NC_)[:, :, i * 128:(i + 1) * 128],
               in_=b.PT[:].rearrange("p (c t) -> p c t", c=NC_, t=128),
               inc=("VE", 1))




def emit_ffn0(S, b, dram):
    wd = b.WS[:, 0:NC_ * 512]
    wu = b.WS[:, 8192:8192 + 4 * 1024]
    S.wait("gpsimd", "PE")
    S.dma("gpsimd", wd.rearrange("p (c i) -> p c i", c=NC_),
          dram["f0_WdT"].rearrange("(c p) i -> p c i", p=128), "WD")
    S.dma("gpsimd", wu.rearrange("p (c h) -> p c h", c=4),
          dram["f0_WuT"].rearrange("(c p) h -> p c h", p=128), "WD")
    S.dma("gpsimd", b.SBF[0:1, SB_BD:SB_BD + 512], dram["f0_bd"][None, :], "WD")
    S.dma("gpsimd", b.SBF[0:8, SB_SEL:SB_SEL + 512], dram["f0_SEL"], "WD")
    S.dma("gpsimd", b.SBF[0:8, SB_BU8:SB_BU8 + 1024], dram["f0_bu"], "WD")
    gwf = b.BUF_B[0:1, 0:16384].bitcast(F32)
    S.wait("sync", "PE")
    S.dma("sync", gwf, dram["f0_gw_flat"], "XD")

    S.wait("tensor", "VE")
    ones_col = b.CBF[:, 0:1]
    grpA = [(b.PC[0:1, 0:512], ones_col, b.BUF_A[:, i * D: i * D + 512])
            for i in range(NT)]
    grpB = [(b.PB[0:1, 0:512], ones_col, b.BUF_A[:, i * D + 512: (i + 1) * D])
            for i in range(NT)]
    S.mm_group(grpA, ("PE", 1))
    S.mm_group(grpB, ("PE", 1))

    S.wait("vector", "PE")
    S.wait("vector", "XD")
    junk = b.SBF[0:1, SB_JUNK:SB_JUNK + 512]
    lA = b.SMALL[0:1, SM_GATE + 40:SM_GATE + 48]
    lB = b.SMALL[0:1, SM_GATE + 48:SM_GATE + 56]
    S.chained("vector", "VE", "memset", b.SMALL[0:1, SM_GATE + 40:SM_GATE + 56], 0.0)
    for e in range(8):
        S.call("vector", "scalar_tensor_tensor", out=junk[0:1, 0:512],
               in0=b.PC[0:1, 0:512], scalar=1.0,
               in1=gwf[0:1, e * 1024:e * 1024 + 512],
               op0=ALU.mult, op1=ALU.mult, accum_out=lA[0:1, e:e + 1],
               inc=(("VE", 1) if e == 7 else None))
        S.call("vector", "scalar_tensor_tensor", out=junk[0:1, 0:512],
               in0=b.PB[0:1, 0:512], scalar=1.0,
               in1=gwf[0:1, e * 1024 + 512:(e + 1) * 1024],
               op0=ALU.mult, op1=ALU.mult, accum_out=lB[0:1, e:e + 1],
               inc=(("VE", 1) if e == 7 else None))
    S.wait("vector", "VE")
    logits = b.SMALL[0:1, SM_GATE:SM_GATE + 8]
    S.chained("vector", "VE", "tensor_tensor", out=logits, in0=lA, in1=lB, op=ALU.add)

    c0 = b.SMALL[0:1, SM_GATE + 8:SM_GATE + 9]
    c1 = b.SMALL[0:1, SM_GATE + 9:SM_GATE + 10]
    c2 = b.SMALL[0:1, SM_GATE + 10:SM_GATE + 11]
    g = b.SMALL[0:1, SM_GATE + 16:SM_GATE + 24]
    eq1 = b.SMALL[0:1, SM_GATE + 24:SM_GATE + 32]
    eq2 = b.SMALL[0:1, SM_GATE + 32:SM_GATE + 40]
    wrow = b.SMALL[0:1, SM_GATE + 56:SM_GATE + 64]
    S.chained("vector", "VE", "reduce_max", out=c0, in_=logits, axis=AX.X)
    S.chained("vector", "VE", "tensor_scalar", out=c0, in0=c0, scalar1=-1.0 / 2048.0,
              scalar2=None, op0=ALU.mult)
    S.wait("scalar", "VE")
    S.chained("scalar", "AC", "activation", out=g, in_=logits, func=AF.Exp,
              scale=1.0 / 2048.0, bias=c0)
    S.wait("vector", "AC")
    S.chained("vector", "VE", "reduce_sum", out=c1, in_=g, axis=AX.X)
    S.chained("vector", "VE", "reciprocal", out=c1, in_=c1)
    S.chained("vector", "VE", "tensor_scalar", out=g, in0=g, scalar1=c1, scalar2=None,
              op0=ALU.mult)
    S.chained("vector", "VE", "reduce_max", out=c0, in_=g, axis=AX.X)
    S.chained("vector", "VE", "tensor_scalar", out=eq1, in0=g, scalar1=c0,
              scalar2=None, op0=ALU.is_equal)
    S.chained("vector", "VE", "scalar_tensor_tensor", out=eq2, in0=eq1, scalar=-1e30,
              in1=g, op0=ALU.mult, op1=ALU.add)
    S.chained("vector", "VE", "reduce_max", out=c1, in_=eq2, axis=AX.X)
    S.chained("vector", "VE", "tensor_scalar", out=eq2, in0=eq2, scalar1=c1,
              scalar2=None, op0=ALU.is_equal)
    S.chained("vector", "VE", "tensor_tensor", out=c2, in0=c0, in1=c1, op=ALU.add)
    S.chained("vector", "VE", "tensor_scalar", out=c2, in0=c2, scalar1=1e-8,
              scalar2=None, op0=ALU.add)
    S.chained("vector", "VE", "reciprocal", out=c2, in_=c2)
    S.chained("vector", "VE", "tensor_scalar", out=c0, in0=c0, scalar1=c2,
              scalar2=None, op0=ALU.mult)
    S.chained("vector", "VE", "tensor_scalar", out=c1, in0=c1, scalar1=c2,
              scalar2=None, op0=ALU.mult)
    S.chained("vector", "VE", "tensor_scalar", out=eq2, in0=eq2, scalar1=c1,
              scalar2=None, op0=ALU.mult)
    S.chained("vector", "VE", "scalar_tensor_tensor", out=wrow, in0=eq1, scalar=c0,
              in1=eq2, op0=ALU.mult, op1=ALU.add)
    wrow_bf = b.SBF[0:1, SB_MISC:SB_MISC + 8]
    S.call("vector", "tensor_copy", out=wrow_bf, in_=wrow, inc=("VE", 1))

    S.wait("tensor", "VE")
    S.transposes([(b.PT[0:8, 512:513], wrow_bf)], b.CBF[0:1, 1:2], ("PE", 1))
    S.wait("vector", "PE")
    wrowT = b.SBF[0:8, SB_MISC + 8:SB_MISC + 9]
    S.call("vector", "tensor_copy", out=wrowT, in_=b.PT[0:8, 512:513],
           inc=("VE", 1))
    S.wait("tensor", "VE")
    S.wait("tensor", "WD")
    for c in range(4):
        S.mm_group([(b.PC[:, 16 + c:17 + c],
                     b.SBF[0:8, SB_SEL + c * 128:SB_SEL + (c + 1) * 128],
                     wrowT)], ("PE", 1))
    S.mm_group([(b.PB[0:1, 0:512], wrowT,
                 b.SBF[0:8, SB_BU8:SB_BU8 + 512])], ("PE", 1))
    S.mm_group([(b.PB[0:1, 512:1024], wrowT,
                 b.SBF[0:8, SB_BU8 + 512:SB_BU8 + 1024])], ("PE", 1))
    S.wait("vector", "PE")
    wcol = b.SMALL[:, SM_WFIN:SM_WFIN + 4]
    S.call("vector", "tensor_copy", out=wcol, in_=b.PC[:, 16:20])
    buw = b.SBF[0:1, SB_BUW:SB_BUW + 1024]
    S.call("vector", "tensor_copy", out=buw[0:1, 0:512], in_=b.PB[0:1, 0:512])
    S.call("vector", "tensor_copy", out=buw[0:1, 512:1024], in_=b.PB[0:1, 512:1024])
    for c in range(4):
        S.call("vector", "tensor_scalar", out=wu[:, c * 1024:(c + 1) * 1024],
               in0=wu[:, c * 1024:(c + 1) * 1024], scalar1=wcol[:, c:c + 1],
               scalar2=None, op0=ALU.mult,
               inc=(("VE", 1) if c == 3 else None))

    h_bf = b.BUF_B[:, 0:4 * SEQ]
    bd_row = b.SBF[0:1, SB_BD:SB_BD + 512]
    ones_row = b.SBF[0:1, SB_ONES:SB_ONES + 512]
    S.wait("tensor", "VE")
    for ts in range(4):
        for c in range(4):
            grp = [(b.PA[:, c * 512:(c + 1) * 512],
                    wd[:, hc * 512 + c * 128: hc * 512 + (c + 1) * 128],
                    b.XFM[:, hc * SEQ + ts * 512: hc * SEQ + (ts + 1) * 512])
                   for hc in range(NC_)]
            grp.append((b.PA[:, c * 512:(c + 1) * 512],
                        bd_row[0:1, c * 128:(c + 1) * 128], ones_row))
            S.mm_group(grp, ("PE", 1))
        S.wait("scalar", "PE")
        S.call("scalar", "activation",
               out=h_bf.rearrange("p (c t) -> p c t", c=4)[:, :, ts * 512:(ts + 1) * 512],
               in_=b.PA[:].rearrange("p (c t) -> p c t", c=4),
               func=AF.Gelu, inc=("AC", 1))
        S.wait("tensor", "AC")
        for j in range(4):
            tt = ts * 4 + j
            for h in range(2):
                grp = [(b.PB[:, h * 512:(h + 1) * 512],
                        h_bf[:, c * SEQ + tt * 128: c * SEQ + (tt + 1) * 128],
                        wu[:, c * 1024 + h * 512: c * 1024 + (h + 1) * 512])
                       for c in range(4)]
                grp.append((b.PB[:, h * 512:(h + 1) * 512],
                            ones_row[0:1, 0:128],
                            buw[0:1, h * 512:(h + 1) * 512]))
                S.mm_group(grp, ("PE", 1))
            S.wait("vector", "PE")
            S.call("vector", "tensor_tensor",
                   out=b.XRES[:, tt * D:(tt + 1) * D],
                   in0=b.XRES[:, tt * D:(tt + 1) * D],
                   in1=b.PB[:, 0:1024], op=ALU.add, inc=("VE", 1))
            S.wait("tensor", "VE")




def emit_shift_col(S, b, shift):
    S.chained("vector", "VE", "tensor_scalar", out=b.CONST[:, C_SH:C_SH + 1],
              in0=b.CONST[:, C_ONE:C_ONE + 1], scalar1=float(-shift * 0.125),
              scalar2=None, op0=ALU.mult)
    S.wait("scalar", "VE")


def emit_attn_heads(S, b, qcols, kw_lo, kw_hi, shift, band_tt=None):
    ones_col = b.CBF[:, 0:1]
    XQ = b.XFM[:, 3072:3072 + NC_ * qcols]
    XP = [b.XFM[:, 7168:7168 + 2 * qcols], b.XFM[:, 12800:12800 + 2 * qcols]]
    XO = b.XFM[:, 8192:8192 + NC_ * qcols]
    REP = b.XFM[:, 12288:12288 + qcols]
    rrow = b.SMALL[0:1, SM_T:SM_T + 1].bitcast(F32) if False else None
    rrowf = b.SMALL[0:1, 96:96 + qcols]
    rrow_bf = b.SBF[0:1, SB_JUNK:SB_JUNK + qcols]
    nkt = kw_hi - kw_lo + 1
    for fc in range(8):
        sc_hist = {}

        def emit_scores(j):
            kt = kw_lo + j
            bk = (j % 2) * 2
            S.wait("tensor", "AC")
            for po in (0, 64):
                S.call("tensor", "matmul",
                       out=b.PA[:, (bk + po // 64) * 512:(bk + po // 64) * 512 + qcols],
                       lhsT=b.BUF_B[po:po + 64, fc * SEQ + kt * 128: fc * SEQ + (kt + 1) * 128],
                       rhs=XQ[po:po + 64, fc * qcols:(fc + 1) * qcols],
                       start=True, stop=True,
                       inc=(("PE", 1) if po else None))
            sc_hist[j] = S.cnt["PE"]
            if band_tt is not None:
                d = kt - band_tt
                if d != 0:
                    S.wait("vector", "PE")
                    msk = C_TRI_HI if d == 1 else C_TRI_LO
                    for po in (0, 64):
                        S.chained("vector", "VE", "tensor_tensor",
                                  out=b.PA[:, (bk + po // 64) * 512:(bk + po // 64) * 512 + qcols],
                                  in0=b.PA[:, (bk + po // 64) * 512:(bk + po // 64) * 512 + qcols],
                                  in1=b.CONST[:, msk:msk + qcols], op=ALU.add)

        def emit_exp(j):
            bk = (j % 2) * 2
            if band_tt is not None:
                S.wait("scalar", "VE")
            S.wait("scalar", "PE", val=sc_hist[j])
            if qcols == 512:
                S.call("scalar", "activation", out=XP[j % 2],
                       in_=b.PA[:, bk * 512:bk * 512 + 1024],
                       func=AF.Exp, scale=0.125, bias=b.CONST[:, C_SH:C_SH + 1],
                       inc=("AC", 1))
            else:
                for po in (0, 64):
                    S.call("scalar", "activation",
                           out=XP[j % 2][:, (po // 64) * qcols:(po // 64 + 1) * qcols],
                           in_=b.PA[:, (bk + po // 64) * 512:(bk + po // 64) * 512 + qcols],
                           func=AF.Exp, scale=0.125, bias=b.CONST[:, C_SH:C_SH + 1],
                           inc=(("AC", 1) if po else None))

        def xp_slice(j, po):
            return XP[j % 2][:, (po // 64) * qcols:(po // 64 + 1) * qcols]

        def emit_out(j):
            kt = kw_lo + j
            if j == 0:
                S.wait("tensor", "VE")
            S.wait("tensor", "AC")
            for po in (0, 64):
                S.call("tensor", "matmul", out=b.PB[po:po + 64, 0:qcols],
                       lhsT=b.BUF_A[:, kt * D + (fc * 2 + po // 64) * 64:
                                    kt * D + (fc * 2 + po // 64) * 64 + 64],
                       rhs=xp_slice(j, po), start=(j == 0), stop=(j == nkt - 1),
                       inc=("PE", 1))
                S.call("tensor", "matmul", out=b.PC[po:po + 1, 0:qcols],
                       lhsT=ones_col, rhs=xp_slice(j, po), start=(j == 0),
                       stop=(j == nkt - 1), inc=("PE", 1))

        for j in range(nkt):
            emit_scores(j)
            if j >= 1:
                emit_exp(j - 1)
                emit_out(j - 1)
        emit_exp(nkt - 1)
        emit_out(nkt - 1)
        S.wait("vector", "PE")
        for po in (0, 64):
            S.chained("vector", "VE", "tensor_copy", out=rrowf[0:1, 0:qcols],
                      in_=b.PC[po:po + 1, 0:qcols])
            S.chained("vector", "VE", "reciprocal", out=rrowf[0:1, 0:qcols],
                      in_=rrowf[0:1, 0:qcols])
            S.chained("vector", "VE", "tensor_copy", out=rrow_bf[0:1, 0:qcols],
                      in_=rrowf[0:1, 0:qcols])
            S.wait("tensor", "VE")
            PTf2 = b.PT[:].bitcast(F32)
            S.call("tensor", "matmul", out=PTf2[po:po + 64, 0:qcols],
                   lhsT=b.SBF[0:1, SB_ONES:SB_ONES + 64],
                   rhs=rrow_bf[0:1, 0:qcols], start=True, stop=True, inc=("PE", 1))
            S.wait("vector", "PE")
            S.call("vector", "tensor_copy", out=REP[po:po + 64],
                   in_=PTf2[po:po + 64, 0:qcols], inc=("VE", 1))
            S.wait("vector", "VE")
            S.call("vector", "tensor_tensor",
                   out=XO[po:po + 64, fc * qcols:(fc + 1) * qcols],
                   in0=b.PB[po:po + 64, 0:qcols], in1=REP[po:po + 64], op=ALU.mult,
                   inc=("VE", 1))


def emit_mha(S, b, dram, pfx, extras=None):
    slot0 = b.WS[:, 0:8192]
    slot1 = b.WS[:, 8192:16384]
    ones_row = b.SBF[0:1, SB_ONES:SB_ONES + 512]
    ident = b.CBF[:, 1:129]
    bias1 = b.SBF[0:1, SB_BIAS1:SB_BIAS1 + 1024]
    bias2 = b.SBF[0:1, SB_BIAS2:SB_BIAS2 + 1024]

    S.wait("gpsimd", "PE")
    S.dma("gpsimd", slot0.rearrange("p (c f) -> p c f", c=NC_),
          dram[pfx + "_Wk"].rearrange("(c p) f -> p c f", p=128), "WD")
    S.dma("gpsimd", slot1.rearrange("p (c f) -> p c f", c=NC_),
          dram[pfx + "_Wv"].rearrange("(c p) f -> p c f", p=128), "WD")
    S.dma("gpsimd", bias1, dram[pfx + "_bk"][None, :], "WD")
    S.dma("gpsimd", bias2, dram[pfx + "_bv"][None, :], "WD")
    S.wait("tensor", "WD")
    S.wait("tensor", "VE")

    ve_hist = []
    for fc in range(NC_):
        for half in range(2):
            if len(ve_hist) >= 2:
                S.wait("tensor", "VE", val=ve_hist[-2])
            for ts2 in range(2):
                ts = half * 2 + ts2
                grp = [(b.PA[:, half * 1024 + ts2 * 512: half * 1024 + (ts2 + 1) * 512],
                        slot0[:, cin * 1024 + fc * 128: cin * 1024 + (fc + 1) * 128],
                        b.XFM[:, cin * SEQ + ts * 512: cin * SEQ + (ts + 1) * 512])
                       for cin in range(NC_)]
                grp.append((b.PA[:, half * 1024 + ts2 * 512: half * 1024 + (ts2 + 1) * 512],
                            bias1[0:1, fc * 128:(fc + 1) * 128], ones_row))
                S.mm_group(grp, ("PE", 1))
            S.wait("vector", "PE")
            S.call("vector", "tensor_copy",
                   out=b.BUF_B[:, fc * SEQ + half * 1024: fc * SEQ + (half + 1) * 1024],
                   in_=b.PA[:, half * 1024:(half + 1) * 1024],
                   inc=("VE", 1))
            ve_hist.append(S.cnt["VE"])
    for tt in range(NT):
        if len(ve_hist) >= 2:
            S.wait("tensor", "VE", val=ve_hist[-2])
        hb = (tt % 2) * 1024
        for h2 in range(2):
            grp = [(b.PA[:, hb + h2 * 512: hb + (h2 + 1) * 512],
                    b.XFM[:, cin * SEQ + tt * 128: cin * SEQ + (tt + 1) * 128],
                    slot1[:, cin * 1024 + h2 * 512: cin * 1024 + (h2 + 1) * 512])
                   for cin in range(NC_)]
            grp.append((b.PA[:, hb + h2 * 512: hb + (h2 + 1) * 512],
                        ones_row[0:1, 0:128], bias2[0:1, h2 * 512:(h2 + 1) * 512]))
            S.mm_group(grp, ("PE", 1))
        S.wait("vector", "PE")
        S.call("vector", "tensor_copy",
               out=b.BUF_A[:, tt * D:(tt + 1) * D], in_=b.PA[:, hb:hb + 1024],
               inc=("VE", 1))
        ve_hist.append(S.cnt["VE"])
    S.wait("tensor", "VE", val=ve_hist[-1])

    S.wait("gpsimd", "PE")
    S.dma("gpsimd", slot0.rearrange("p (c f) -> p c f", c=NC_),
          dram[pfx + "_Wq"].rearrange("(c p) f -> p c f", p=128), "WD")
    S.dma("gpsimd", slot1.rearrange("p (c f) -> p c f", c=NC_),
          dram[pfx + "_Wo"].rearrange("(c p) f -> p c f", p=128), "WD")
    S.wait("vector", "PE")
    S.dma("gpsimd", bias1, dram[pfx + "_bq"][None, :], "WD")
    S.dma("gpsimd", bias2, dram[pfx + "_bo"][None, :], "WD")
    S.wait("tensor", "WD")

    XT = b.XFM[:, 0:1024]
    XQF = b.XFM[:, 1024:2048]
    XQ = b.XFM[:, 3072:3072 + NC_ * 512]
    XO = b.XFM[:, 8192:8192 + NC_ * 512]
    sM = b.SMALL[:, SM_M:SM_M + NT]
    sR = b.SMALL[:, SM_R:SM_R + NT]
    emit_shift_col(S, b, 16.0)
    for sl in range(4):
        for tl in range(4):
            tt = sl * 4 + tl
            S.wait("vector", "PE")
            S.call("vector", "tensor_scalar", out=XT, in0=b.XRES[:, tt * D:(tt + 1) * D],
                   scalar1=sM[:, tt:tt + 1], scalar2=sR[:, tt:tt + 1],
                   op0=ALU.subtract, op1=ALU.mult, inc=("VE", 1))
            S.wait("tensor", "VE")
            S.transposes([(b.PT[:, c * 128:(c + 1) * 128], XT[:, c * 128:(c + 1) * 128])
                          for c in range(NC_)], ident, ("PE", 1))
            S.wait("vector", "PE")
            S.call("vector", "tensor_copy", out=XQF, in_=b.PT[:], inc=("VE", 1))
            S.wait("tensor", "VE")
            for fc in range(NC_):
                grp = [(b.PA[:, fc * 128:(fc + 1) * 128],
                        slot0[:, cin * 1024 + fc * 128: cin * 1024 + (fc + 1) * 128],
                        XQF[:, cin * 128:(cin + 1) * 128]) for cin in range(NC_)]
                grp.append((b.PA[:, fc * 128:(fc + 1) * 128],
                            bias1[0:1, fc * 128:(fc + 1) * 128], ones_row[0:1, 0:128]))
                S.mm_group(grp, ("PE", 1))
            S.wait("vector", "PE")
            S.call("vector", "tensor_copy",
                   out=XQ.rearrange("p (c q) -> p c q", c=NC_)[:, :, tl * 128:(tl + 1) * 128],
                   in_=b.PA[:, 0:1024].rearrange("p (c q) -> p c q", c=NC_),
                   inc=("VE", 1))
            S.wait("tensor", "VE")
        emit_attn_heads(S, b, 512, 0, NT - 1, 16.0)
        S.wait("tensor", "VE")
        op_hist = []
        for tl in range(4):
            tt = sl * 4 + tl
            hb = (tl % 2) * 1024
            if len(op_hist) >= 2:
                S.wait("tensor", "VE", val=op_hist[-2])
            for h2 in range(2):
                grp = [(b.PA[:, hb + h2 * 512: hb + (h2 + 1) * 512],
                        XO[:, cin * 512 + tl * 128: cin * 512 + tl * 128 + 128],
                        slot1[:, cin * 1024 + h2 * 512: cin * 1024 + (h2 + 1) * 512])
                       for cin in range(NC_)]
                grp.append((b.PA[:, hb + h2 * 512: hb + (h2 + 1) * 512],
                            ones_row[0:1, 0:128], bias2[0:1, h2 * 512:(h2 + 1) * 512]))
                S.mm_group(grp, ("PE", 1))
            S.wait("vector", "PE")
            S.call("vector", "tensor_tensor", out=b.XRES[:, tt * D:(tt + 1) * D],
                   in0=b.XRES[:, tt * D:(tt + 1) * D], in1=b.PA[:, hb:hb + 1024],
                   op=ALU.add, inc=("VE", 1))
            op_hist.append(S.cnt["VE"])
        S.wait("tensor", "VE", val=op_hist[-1])


SMX_S = SM_X
SMX_C1 = SM_X + 16
SMX_C2 = SM_X + 32
SMX_F = SM_X + 48


def emit_ln2cols(S, b, want_resid=False):
    sA = b.SMALL[:, SM_W:SM_W + NT]
    sR = b.SMALL[:, SM_R:SM_R + NT]
    sM = b.SMALL[:, SM_M:SM_M + NT]
    sS2 = b.SMALL[:, SMX_S:SMX_S + NT]
    S.chained("vector", "VE", "reciprocal", out=sS2, in_=sA)
    S.chained("vector", "VE", "tensor_scalar", out=sS2, in0=sS2,
              scalar1=-LN_EPS, scalar2=1.0 + LN_EPS, op0=ALU.mult, op1=ALU.add)
    S.wait("scalar", "VE")
    S.chained("scalar", "AC", "activation", out=sS2, in_=sS2, func=AF.Ln, scale=1.0)
    S.call("scalar", "activation", out=sS2, in_=sS2, func=AF.Exp,
           scale=-0.5, inc=("AC", 1))
    S.wait("vector", "AC")
    S.chained("vector", "VE", "tensor_tensor", out=sS2, in0=sS2, in1=sR,
              op=ALU.mult)
    if want_resid:
        S.chained("vector", "VE", "tensor_scalar",
                  out=b.SMALL[:, SMX_C1:SMX_C1 + NT], in0=sR, scalar1=1.0,
                  scalar2=None, op0=ALU.add)
        S.chained("vector", "VE", "tensor_tensor",
                  out=b.SMALL[:, SMX_C2:SMX_C2 + NT], in0=sM, in1=sR, op=ALU.mult)


def _emit_rms(S, b, tmp, junk, qg_col=None, eighth_bias=None):
    fcol = b.SMALL[:, SMX_F:SMX_F + 16]
    S.chained("vector", "VE", "memset", fcol, 0.0)
    for h in range(16):
        sl = tmp[:, h * 64:(h + 1) * 64]
        S.call("vector", "scalar_tensor_tensor", out=junk[:, 0:64], in0=sl,
               scalar=1.0, in1=sl, op0=ALU.mult, op1=ALU.mult,
               accum_out=fcol[:, h:h + 1],
               inc=(("VE", 1) if h == 15 else None))
    S.wait("vector", "VE")
    S.wait("scalar", "VE")
    S.chained("scalar", "AC", "activation", out=fcol, in_=fcol, func=AF.Ln, scale=1.0)
    S.call("scalar", "activation", out=fcol, in_=fcol, func=AF.Exp,
           scale=0.5, inc=("AC", 1))
    S.wait("vector", "AC")
    S.chained("vector", "VE", "tensor_scalar", out=fcol, in0=fcol,
              scalar1=0.125, scalar2=1e-8, op0=ALU.mult, op1=ALU.max)
    S.chained("vector", "VE", "reciprocal", out=fcol, in_=fcol)
    for h in range(16):
        sl = tmp[:, h * 64:(h + 1) * 64]
        S.call("vector", "tensor_scalar", out=sl, in0=sl,
               scalar1=fcol[:, h:h + 1], scalar2=None, op0=ALU.mult,
               inc=(("VE", 1) if h == 15 else None))
    S.wait("vector", "VE")


def emit_mha2(S, b, dram, pfx, rms=False, band=False, inner_beta=True,
              out_bias=True, inner_residual=False, stop_after=None):
    slot0 = b.WS[:, 0:8192]
    slot1 = b.WS[:, 8192:16384]
    ones_row = b.SBF[0:1, SB_ONES:SB_ONES + 512]
    ident = b.CBF[:, 1:129]
    bias1 = b.SBF[0:1, SB_BIAS1:SB_BIAS1 + 1024]
    bias2 = b.SBF[0:1, SB_BIAS2:SB_BIAS2 + 1024]
    gcol = b.CONST[:, C_G:C_G + 8]
    bcol = b.CBF[:, CB_B8:CB_B8 + 8]
    junk = b.SBF[:, SB_JUNK:SB_JUNK + 512]

    def load_fold(wname, bias_row, with_beta, with_gamma):
        slot = slot0 if wname in ("Wk", "Wq") else slot1
        S.wait("gpsimd", "PE")
        S.wait("gpsimd", "VE")
        S.dma("gpsimd", slot.rearrange("p (c f) -> p c f", c=NC_),
              dram[pfx + "_" + wname].rearrange("(c p) f -> p c f", p=128), "WD")
        S.wait("tensor", "WD")
        if with_beta:
            for hf in range(2):
                grp = [(b.PC[0:1, 0:512],
                        bcol[:, c:c + 1],
                        slot[:, c * 1024 + hf * 512: c * 1024 + (hf + 1) * 512])
                       for c in range(NC_)]
                S.mm_group(grp, ("PE", 1))
                S.wait("vector", "PE")
                S.call("vector", "tensor_copy",
                       out=bias_row[0:1, hf * 512:(hf + 1) * 512],
                       in_=b.PC[0:1, 0:512], inc=("VE", 1))
                S.wait("tensor", "VE")
        if with_gamma:
            for c in range(NC_):
                S.call("vector", "tensor_scalar",
                       out=slot[:, c * 1024:(c + 1) * 1024],
                       in0=slot[:, c * 1024:(c + 1) * 1024],
                       scalar1=gcol[:, c:c + 1], scalar2=None, op0=ALU.mult,
                       inc=(("VE", 1) if c == NC_ - 1 else None))
            S.wait("tensor", "VE")

    load_fold("Wk", bias1, inner_beta, True)
    ktmps = [b.BUF_A[:, 0:1024], b.BUF_A[:, 1024:2048]]
    kv_hist = []
    for kt in range(NT):
        hb = (kt % 2) * 1024
        ktmp = ktmps[kt % 2]
        if len(kv_hist) >= 2:
            S.wait("tensor", "VE", val=kv_hist[-2])
        for h2 in range(2):
            grp = [(b.PA[:, hb + h2 * 512: hb + (h2 + 1) * 512],
                    b.XFM[:, cin * SEQ + kt * 128: cin * SEQ + (kt + 1) * 128],
                    slot0[:, cin * 1024 + h2 * 512: cin * 1024 + (h2 + 1) * 512])
                   for cin in range(NC_)]
            if inner_beta:
                grp.append((b.PA[:, hb + h2 * 512: hb + (h2 + 1) * 512],
                            ones_row[0:1, 0:128],
                            bias1[0:1, h2 * 512:(h2 + 1) * 512]))
            S.mm_group(grp, ("PE", 1))
        S.wait("vector", "PE")
        S.call("vector", "tensor_copy", out=ktmp, in_=b.PA[:, hb:hb + 1024],
               inc=("VE", 1))
        kv_hist.append(S.cnt["VE"])
        if rms:
            _emit_rms(S, b, ktmp, junk)
        S.wait("tensor", "VE")
        S.transposes([(b.PT[:, c * 128:(c + 1) * 128], ktmp[:, c * 128:(c + 1) * 128])
                      for c in range(NC_)], ident, ("PE", 1))
        S.wait("vector", "PE")
        S.call("vector", "tensor_copy",
               out=b.BUF_B.rearrange("p (c t) -> p c t", c=NC_)[:, :, kt * 128:(kt + 1) * 128],
               in_=b.PT[:].rearrange("p (c t) -> p c t", c=NC_, t=128),
               inc=("VE", 1))
    if rms:
        S.wait("vector", "PE")
        S.call("vector", "tensor_scalar", out=b.BUF_B[:], in0=b.BUF_B[:],
               scalar1=b.CONST[:, C_KG:C_KG + 1], scalar2=None, op0=ALU.mult,
               inc=("VE", 1))
    if stop_after == "k":
        return
    load_fold("Wv", bias2, inner_beta, True)
    for tt in range(NT):
        for h2 in range(2):
            grp = [(b.PB[:, h2 * 512:(h2 + 1) * 512],
                    b.XFM[:, cin * SEQ + tt * 128: cin * SEQ + (tt + 1) * 128],
                    slot1[:, cin * 1024 + h2 * 512: cin * 1024 + (h2 + 1) * 512])
                   for cin in range(NC_)]
            if inner_beta:
                grp.append((b.PB[:, h2 * 512:(h2 + 1) * 512],
                            ones_row[0:1, 0:128],
                            bias2[0:1, h2 * 512:(h2 + 1) * 512]))
            S.mm_group(grp, ("PE", 1))
        S.wait("vector", "PE")
        S.call("vector", "tensor_copy", out=b.BUF_A[:, tt * D:(tt + 1) * D],
               in_=b.PB[:], inc=("VE", 1))
        S.wait("tensor", "VE")

    if stop_after == "v":
        return
    load_fold("Wq", bias1, inner_beta, True)
    S.wait("gpsimd", "PE")
    S.dma("gpsimd", slot1.rearrange("p (c f) -> p c f", c=NC_),
          dram[pfx + "_Wo"].rearrange("(c p) f -> p c f", p=128), "WD")
    if out_bias:
        S.wait("gpsimd", "VE")
        S.dma("gpsimd", bias2, dram[pfx + "_bo"][None, :], "WD")
    S.wait("tensor", "WD")

    qcols = 128 if band else 512
    XT = b.XFM[:, 0:1024]
    XQF = b.XFM[:, 1024:2048]
    QTMP = b.XFM[:, 2048:3072]
    XQ = b.XFM[:, 3072:3072 + NC_ * qcols]
    XO = b.XFM[:, 8192:8192 + NC_ * qcols]
    sM = b.SMALL[:, SM_M:SM_M + NT]
    sS2 = b.SMALL[:, SMX_S:SMX_S + NT]
    emit_shift_col(S, b, 60.0 if band else 8.0)
    tiles_per_slab = 1 if band else 4
    for sl in range(NT // tiles_per_slab):
        for tl in range(tiles_per_slab):
            tt = sl * tiles_per_slab + tl
            S.wait("vector", "PE")
            S.call("vector", "tensor_scalar", out=XT, in0=b.XRES[:, tt * D:(tt + 1) * D],
                   scalar1=sM[:, tt:tt + 1], scalar2=sS2[:, tt:tt + 1],
                   op0=ALU.subtract, op1=ALU.mult, inc=("VE", 1))
            S.wait("tensor", "VE")
            S.transposes([(b.PT[:, c * 128:(c + 1) * 128], XT[:, c * 128:(c + 1) * 128])
                          for c in range(NC_)], ident, ("PE", 1))
            S.wait("vector", "PE")
            S.call("vector", "tensor_copy", out=XQF, in_=b.PT[:], inc=("VE", 1))
            S.wait("tensor", "VE")
            for h2 in range(2):
                grp = [(b.PA[:, h2 * 512:(h2 + 1) * 512],
                        XQF[:, cin * 128:(cin + 1) * 128],
                        slot0[:, cin * 1024 + h2 * 512: cin * 1024 + (h2 + 1) * 512])
                       for cin in range(NC_)]
                if inner_beta:
                    grp.append((b.PA[:, h2 * 512:(h2 + 1) * 512],
                                ones_row[0:1, 0:128],
                                bias1[0:1, h2 * 512:(h2 + 1) * 512]))
                S.mm_group(grp, ("PE", 1))
            S.wait("vector", "PE")
            S.call("vector", "tensor_copy", out=QTMP, in_=b.PA[:, 0:1024],
                   inc=("VE", 1))
            S.wait("tensor", "VE")
            if rms:
                _emit_rms(S, b, QTMP, junk)
            S.wait("tensor", "VE")
            S.transposes([(b.PT[:, c * 128:(c + 1) * 128], QTMP[:, c * 128:(c + 1) * 128])
                          for c in range(NC_)], ident, ("PE", 1))
            S.wait("vector", "PE")
            dstq = XQ.rearrange("p (c q) -> p c q", c=NC_)[:, :, tl * 128:(tl + 1) * 128]
            if rms:
                S.call("vector", "tensor_scalar", out=dstq,
                       in0=b.PT[:].rearrange("p (c q) -> p c q", c=NC_, q=128),
                       scalar1=b.CONST[:, C_QG:C_QG + 1], scalar2=None,
                       op0=ALU.mult, inc=("VE", 1))
            else:
                S.call("vector", "tensor_copy", out=dstq,
                       in_=b.PT[:].rearrange("p (c q) -> p c q", c=NC_, q=128),
                       inc=("VE", 1))
            S.wait("tensor", "VE")
        klo = max(0, sl - 1) if band else 0
        khi = min(NT - 1, sl + 1) if band else NT - 1
        emit_attn_heads(S, b, qcols, klo, khi, 60.0 if band else 8.0,
                        band_tt=(sl if band else None))
        S.wait("tensor", "VE")
        op_hist = []
        for tl in range(tiles_per_slab):
            tt = sl * tiles_per_slab + tl
            hb = (tl % 2) * 1024
            if len(op_hist) >= 2:
                S.wait("tensor", "VE", val=op_hist[-2])
            for h2 in range(2):
                grp = [(b.PA[:, hb + h2 * 512: hb + (h2 + 1) * 512],
                        XO[:, cin * qcols + tl * 128: cin * qcols + tl * 128 + 128],
                        slot1[:, cin * 1024 + h2 * 512: cin * 1024 + (h2 + 1) * 512])
                       for cin in range(NC_)]
                if out_bias:
                    grp.append((b.PA[:, hb + h2 * 512: hb + (h2 + 1) * 512],
                                ones_row[0:1, 0:128],
                                bias2[0:1, h2 * 512:(h2 + 1) * 512]))
                S.mm_group(grp, ("PE", 1))
            S.wait("vector", "PE")
            if inner_residual:
                S.chained("vector", "VE", "tensor_scalar",
                          out=b.XRES[:, tt * D:(tt + 1) * D],
                          in0=b.XRES[:, tt * D:(tt + 1) * D],
                          scalar1=b.SMALL[:, SMX_C1 + tt:SMX_C1 + tt + 1],
                          scalar2=b.SMALL[:, SMX_C2 + tt:SMX_C2 + tt + 1],
                          op0=ALU.mult, op1=ALU.subtract)
            S.call("vector", "tensor_tensor", out=b.XRES[:, tt * D:(tt + 1) * D],
                   in0=b.XRES[:, tt * D:(tt + 1) * D], in1=b.PA[:, hb:hb + 1024],
                   op=ALU.add, inc=("VE", 1))
            op_hist.append(S.cnt["VE"])
        if op_hist:
            S.wait("tensor", "VE", val=op_hist[-1])




def emit_aa0_cols(S, b):
    sA = b.SMALL[:, SM_W:SM_W + NT]
    sR = b.SMALL[:, SM_R:SM_R + NT]
    sS2 = b.SMALL[:, SMX_S:SMX_S + NT]
    S.chained("vector", "VE", "reciprocal", out=sS2, in_=sA)
    S.chained("vector", "VE", "tensor_scalar", out=sS2, in0=sS2,
              scalar1=-LN_EPS * D, scalar2=float(D), op0=ALU.mult, op1=ALU.add)
    S.wait("scalar", "VE")
    S.chained("scalar", "AC", "activation", out=sS2, in_=sS2, func=AF.Ln, scale=1.0)
    S.call("scalar", "activation", out=sS2, in_=sS2, func=AF.Exp,
           scale=0.5, inc=("AC", 1))
    S.wait("vector", "AC")
    S.chained("vector", "VE", "tensor_scalar", out=sS2, in0=sS2,
              scalar1=1e-12, scalar2=None, op0=ALU.max)
    S.chained("vector", "VE", "reciprocal", out=sS2, in_=sS2)
    S.chained("vector", "VE", "tensor_tensor", out=sS2, in0=sS2, in1=sR,
              op=ALU.mult)
    S.chained("vector", "VE", "tensor_scalar", out=sS2, in0=sS2,
              scalar1=32.0, scalar2=None, op0=ALU.mult)


def emit_attn_a0(S, b, dram):
    slot0 = b.WS[:, 0:8192]
    slot1 = b.WS[:, 8192:16384]
    ident = b.CBF[:, 1:129]
    ones_col = b.CBF[:, 0:1]
    ones_row = b.SBF[0:1, SB_ONES:SB_ONES + 512]
    gcol = b.CONST[:, C_G:C_G + 8]
    QT = b.BUF_A[:, 0:1024]
    KT = b.BUF_A[:, 1024:2048]
    SQT = b.BUF_A[:, 2048:3072]
    simacc = b.BUF_A[0:64, 3072:5120].bitcast(F32)
    sqq = b.SBF[0:1, 512:2560].bitcast(F32)
    sqk = b.SBF[0:1, 2560:4608].bitcast(F32)

    def load_w(wname, slot, fold):
        S.wait("gpsimd", "PE")
        S.wait("gpsimd", "VE")
        S.dma("gpsimd", slot.rearrange("p (c f) -> p c f", c=NC_),
              dram["aa0_" + wname].rearrange("(c p) f -> p c f", p=128), "WD")
        S.wait("vector", "WD")
        if fold:
            for c in range(NC_):
                S.call("vector", "tensor_scalar",
                       out=slot[:, c * 1024:(c + 1) * 1024],
                       in0=slot[:, c * 1024:(c + 1) * 1024],
                       scalar1=gcol[:, c:c + 1], scalar2=None, op0=ALU.mult,
                       inc=(("VE", 1) if c == NC_ - 1 else None))
        S.wait("tensor", "WD")
        S.wait("tensor", "VE")

    load_w("Wq", slot0, True)
    load_w("Wk", slot1, True)
    S.chained("vector", "VE", "memset", simacc, 0.0)
    S.chained("vector", "VE", "memset", sqq, 0.0)
    S.chained("vector", "VE", "memset", sqk, 0.0)
    S.wait("tensor", "VE")

    for tt in range(NT):
        for h2 in range(2):
            S.mm_group([(b.PB[:, h2 * 512:(h2 + 1) * 512],
                         b.XFM[:, cin * SEQ + tt * 128: cin * SEQ + (tt + 1) * 128],
                         slot0[:, cin * 1024 + h2 * 512: cin * 1024 + (h2 + 1) * 512])
                        for cin in range(NC_)], ("PE", 1))
            S.mm_group([(b.PA[:, h2 * 512:(h2 + 1) * 512],
                         b.XFM[:, cin * SEQ + tt * 128: cin * SEQ + (tt + 1) * 128],
                         slot1[:, cin * 1024 + h2 * 512: cin * 1024 + (h2 + 1) * 512])
                        for cin in range(NC_)], ("PE", 1))
        S.wait("vector", "PE")
        S.call("vector", "tensor_copy", out=QT, in_=b.PB[:])
        S.call("vector", "tensor_copy", out=KT, in_=b.PA[:, 0:1024], inc=("VE", 1))
        S.wait("scalar", "VE")
        S.chained("scalar", "AC", "activation", out=SQT, in_=QT, func=AF.Square)
        S.wait("tensor", "VE")
        for h in range(16):
            S.mm_group([(b.PA[0:64, 1024 + h * 64:1024 + (h + 1) * 64],
                         QT[:, h * 64:(h + 1) * 64], KT[:, h * 64:(h + 1) * 64])],
                       ("PE", 1))
        S.wait("tensor", "AC")
        for hf in range(2):
            S.mm_group([(b.PC[0:1, 0:512], ones_col,
                         SQT[:, hf * 512:(hf + 1) * 512])], ("PE", 1))
            S.wait("vector", "PE")
            S.call("vector", "tensor_tensor", out=sqq[0:1, hf * 512:(hf + 1) * 512],
                   in0=sqq[0:1, hf * 512:(hf + 1) * 512], in1=b.PC[0:1, 0:512],
                   op=ALU.add, inc=("VE", 1))
            S.wait("tensor", "VE")
        S.wait("scalar", "PE")
        S.chained("scalar", "AC", "activation", out=SQT, in_=KT, func=AF.Square)
        S.wait("tensor", "AC")
        for hf in range(2):
            S.mm_group([(b.PC[0:1, 0:512], ones_col,
                         SQT[:, hf * 512:(hf + 1) * 512])], ("PE", 1))
            S.wait("vector", "PE")
            S.call("vector", "tensor_tensor", out=sqk[0:1, hf * 512:(hf + 1) * 512],
                   in0=sqk[0:1, hf * 512:(hf + 1) * 512], in1=b.PC[0:1, 0:512],
                   op=ALU.add, inc=("VE", 1))
            S.wait("tensor", "VE")
        S.wait("vector", "PE")
        S.call("vector", "tensor_tensor", out=simacc, in0=simacc,
               in1=b.PA[0:64, 1024:2048], op=ALU.add, inc=("VE", 1))
        S.wait("tensor", "VE")

    load_w("Wv", slot0, True)
    for fc in range(NC_):
        for ts in range(4):
            S.mm_group([(b.PA[:, ts * 512:(ts + 1) * 512],
                         slot0[:, cin * 1024 + fc * 128: cin * 1024 + (fc + 1) * 128],
                         b.XFM[:, cin * SEQ + ts * 512: cin * SEQ + (ts + 1) * 512])
                        for cin in range(NC_)], ("PE", 1))
        S.wait("vector", "PE")
        S.call("vector", "tensor_copy",
               out=b.BUF_B[:, fc * SEQ:(fc + 1) * SEQ], in_=b.PA[:],
               inc=("VE", 1))
        S.wait("tensor", "VE")
    load_w("Wo", slot1, False)

    S.wait("scalar", "VE")
    S.chained("scalar", "AC", "activation", out=sqk, in_=sqk, func=AF.Ln, scale=1.0)
    S.call("scalar", "activation", out=sqk, in_=sqk, func=AF.Exp,
           scale=0.5, inc=("AC", 1))
    S.chained("scalar", "AC", "activation", out=sqq, in_=sqq, func=AF.Ln, scale=1.0)
    S.call("scalar", "activation", out=sqq, in_=sqq, func=AF.Exp,
           scale=0.5, inc=("AC", 1))
    S.wait("vector", "AC")
    S.chained("vector", "VE", "tensor_scalar", out=sqk, in0=sqk,
              scalar1=1e-12, scalar2=None, op0=ALU.max)
    S.chained("vector", "VE", "reciprocal", out=sqk, in_=sqk)
    S.chained("vector", "VE", "tensor_scalar", out=sqq, in0=sqq,
              scalar1=1e-12, scalar2=None, op0=ALU.max)
    S.chained("vector", "VE", "reciprocal", out=sqq, in_=sqq)
    et = b.SMALL[0:1, SM_T:SM_T + 16]
    S.dma("sync", et, dram["aa0_temp"], "XD")
    S.wait("scalar", "XD")
    S.wait("scalar", "VE")
    S.chained("scalar", "AC", "activation", out=et, in_=et, func=AF.Exp, scale=1.0)
    S.wait("vector", "AC")
    for h in range(16):
        S.call("vector", "tensor_scalar", out=sqk[0:1, h * 64:(h + 1) * 64],
               in0=sqk[0:1, h * 64:(h + 1) * 64], scalar1=et[0:1, h:h + 1],
               scalar2=8.0, op0=ALU.mult, op1=ALU.mult,
               inc=(("VE", 1) if h == 15 else None))
    S.wait("vector", "VE")
    PTf = b.PT[:].bitcast(F32)
    S.wait("tensor", "VE")
    S.transposes([(PTf[0:64, h:h + 1], sqq[0:1, h * 64:(h + 1) * 64])
                  for h in range(16)], b.CONST[0:1, C_ONE:C_ONE + 1], ("PE", 1))
    S.wait("vector", "PE")
    rqP = b.SMALL[0:64, SM_GATE:SM_GATE + 16]
    S.call("vector", "tensor_copy", out=rqP, in_=PTf[0:64, 0:16], inc=("VE", 1))
    XAT = b.XFM[:, 0:2048]
    mx = b.SMALL[:, SM_S:SM_S + 1]
    mxn = b.SMALL[:, SM_S + 1:SM_S + 2]
    rs = b.SMALL[:, SM_S + 2:SM_S + 3]
    rr = b.SMALL[:, SM_S + 3:SM_S + 4]
    prb = b.XFM[:, 2048:2112]
    for h in range(16):
        xcol = (h // 2) * 64 + (h % 2) * 512
        sl = simacc[0:64, h * 64:(h + 1) * 64]
        S.wait("vector", "AC")
        S.chained("vector", "VE", "tensor_scalar", out=sl, in0=sl,
                  scalar1=rqP[0:64, h:h + 1], scalar2=None, op0=ALU.mult)
        S.wait("tensor", "VE")
        S.mm_group([(b.PC[0:64, 0:64], b.CONST[0:1, C_ONER:C_ONER + 64],
                     sqk[0:1, h * 64:(h + 1) * 64])], ("PE", 1))
        S.wait("vector", "PE")
        S.chained("vector", "VE", "tensor_tensor", out=sl, in0=sl,
                  in1=b.PC[0:64, 0:64], op=ALU.mult)
        S.chained("vector", "VE", "reduce_max", out=mx[0:64], in_=sl, axis=AX.X)
        S.chained("vector", "VE", "tensor_scalar", out=mxn[0:64],
                  in0=mx[0:64], scalar1=-1.0, scalar2=None, op0=ALU.mult)
        S.chained("vector", "VE", "memset", rs[0:64], 0.0)
        S.wait("scalar", "VE")
        S.call("scalar", "activation", out=prb[0:64], in_=sl, func=AF.Exp,
               scale=1.0, bias=mxn[0:64], accum_out=rs[0:64], inc=("AC", 1))
        S.wait("vector", "AC")
        S.chained("vector", "VE", "reciprocal", out=rr[0:64], in_=rs[0:64])
        S.chained("vector", "VE", "tensor_scalar", out=prb[0:64],
                  in0=prb[0:64], scalar1=rr[0:64], scalar2=None, op0=ALU.mult)
        S.wait("tensor", "VE")
        S.transposes([(b.PT[0:64, 0:64], prb[0:64])], ident[0:64, 0:64], ("PE", 1))
        S.wait("vector", "PE")
        S.call("vector", "tensor_copy", out=XAT[0:64, xcol:xcol + 64],
               in_=b.PT[0:64, 0:64], inc=("VE", 1))
        S.wait("tensor", "VE")
    S.wait("gpsimd", "VE")
    S.dma("gpsimd", b.XFM[64:128, 1024:1536], XAT[0:64, 512:1024], "WD")
    S.wait("tensor", "WD")
    XAT2 = b.XFM[64:128, 1024:1536]

    for fc in range(NC_):
        for po in (0, 64):
            h = fc * 2 + po // 64
            at = XAT[0:64, fc * 64:fc * 64 + 64] if po == 0 else \
                 XAT2[:, fc * 64:fc * 64 + 64]
            for ts in range(4):
                S.mm_group([(b.PA[po:po + 64, ts * 512:(ts + 1) * 512],
                             at,
                             b.BUF_B[po:po + 64, fc * SEQ + ts * 512: fc * SEQ + (ts + 1) * 512])],
                           ("PE", 1))
        S.wait("vector", "PE")
        S.call("vector", "tensor_copy", out=b.BUF_A[:, fc * SEQ:(fc + 1) * SEQ],
               in_=b.PA[:], inc=("VE", 1))
        S.wait("tensor", "VE")
    for tt in range(NT):
        for h2 in range(2):
            S.mm_group([(b.PB[:, h2 * 512:(h2 + 1) * 512],
                         b.BUF_A[:, cin * SEQ + tt * 128: cin * SEQ + (tt + 1) * 128],
                         slot1[:, cin * 1024 + h2 * 512: cin * 1024 + (h2 + 1) * 512])
                        for cin in range(NC_)], ("PE", 1))
        S.wait("vector", "PE")
        S.call("vector", "tensor_tensor", out=b.XRES[:, tt * D:(tt + 1) * D],
               in0=b.XRES[:, tt * D:(tt + 1) * D], in1=b.PB[:], op=ALU.add,
               inc=("VE", 1))
        S.wait("tensor", "VE")




def emit_moe1_gate(S, b, dram):
    gwf = b.BUF_B[:, 4096:4608].bitcast(F32)
    S.wait("sync", "PE")
    S.wait("sync", "VE")
    S.dma("sync", gwf.rearrange("p (c e) -> p c e", c=NC_),
          dram["m1_gwT"].rearrange("(c p) e -> p c e", p=128), "XD")
    gb_bf = b.SBF[0:1, SB_MISC:SB_MISC + 32]
    S.wait("gpsimd", "PE")
    S.dma("gpsimd", gb_bf, dram["m1_gbias"][None, :], "WD")
    S.wait("tensor", "WD")
    S.wait("tensor", "XD")
    S.wait("tensor", "VE")
    S.mm_group([(b.PC[:, 0:32], b.SBF[0:1, SB_ONES:SB_ONES + 128], gb_bf)],
               ("PE", 1))
    S.wait("vector", "PE")
    gbr = b.SMALL[:, SM_X:SM_X + 32]
    S.call("vector", "tensor_copy", out=gbr, in_=b.PC[:, 0:32], inc=("VE", 1))
    S.wait("tensor", "VE")
    S.mm_group([(b.PC[0:1, 32:64], b.CONST[:, C_ONE:C_ONE + 1],
                 gwf[:, c * 32:(c + 1) * 32]) for c in range(NC_)], ("PE", 1))
    S.wait("vector", "PE")
    gsrow = b.SMALL[0:1, SM_X + 32:SM_X + 64]
    S.call("vector", "tensor_copy", out=gsrow, in_=b.PC[0:1, 32:64], inc=("VE", 1))
    S.wait("tensor", "VE")
    S.mm_group([(b.PC[:, 32:64], b.CONST[0:1, C_ONER:C_ONER + 128], gsrow)],
               ("PE", 1))
    S.wait("vector", "PE")
    gsr = b.SMALL[:, 736:768]
    S.call("vector", "tensor_copy", out=gsr, in_=b.PC[:, 32:64])
    negR = b.SMALL[:, SM_W:SM_W + NT]
    S.call("vector", "tensor_scalar", out=negR, in0=b.SMALL[:, SM_R:SM_R + NT],
           scalar1=-1.0, scalar2=None, op0=ALU.mult, inc=("VE", 1))
    S.wait("tensor", "VE")
    xT = b.BUF_B[:, 0:2048].bitcast(F32)
    SC = b.SMALL[:, SM_GATE:SM_GATE + 32]
    SCC = b.SMALL[:, SM_GATE + 32:SM_GATE + 64]
    T32 = b.SMALL[:, SM_S:SM_S + 32]
    MO = b.BUF_B[:, 2560:2624].bitcast(F32)
    m1c = b.SMALL[:, SM_T:SM_T + 1]
    m2c = b.SMALL[:, SM_T + 1:SM_T + 2]
    gsum = b.SMALL[:, SM_T + 2:SM_T + 6]
    gthr = b.SMALL[:, SM_T + 6:SM_T + 7]
    gmask = b.SMALL[:, SM_T + 8:SM_T + 12]
    thr8 = b.SMALL[:, SM_T + 12:SM_T + 13]
    den = b.SMALL[:, SM_T + 13:SM_T + 14]
    SC16 = b.BUF_B[:, 6144:7168].bitcast(F32)
    ac_hist = []
    for tt in range(NT):
        PTf = b.PT[:].bitcast(F32)
        lp = b.PC[:, 0:32] if tt % 2 == 0 else b.PB[:, 0:32]
        if len(ac_hist) >= 2:
            S.wait("tensor", "AC", val=ac_hist[-2])
        for half in range(2):
            S.wait("tensor", "VE")
            S.transposes([(PTf[:, c * 128:(c + 1) * 128],
                           b.XRES[:, tt * D + (half * 4 + c) * 128:
                                  tt * D + (half * 4 + c + 1) * 128])
                          for c in range(4)], b.CONST[:, C_IDF32:C_IDF32 + 128],
                         ("PE", 1))
            S.wait("vector", "PE")
            S.call("vector", "tensor_copy",
                   out=xT[:, half * 512:(half + 1) * 512], in_=PTf[:, 0:512],
                   inc=("VE", 1))
        S.wait("tensor", "VE")
        grp = [(lp, xT[:, cin * 128:(cin + 1) * 128],
                gwf[:, cin * 32:(cin + 1) * 32]) for cin in range(NC_)]
        S.mm_group(grp, ("PE", 1))
        S.wait("vector", "PE")
        S.chained("vector", "VE", "scalar_tensor_tensor", out=SCC, in0=gsr,
                  scalar=b.SMALL[:, SM_M + tt:SM_M + tt + 1], in1=lp,
                  op0=ALU.mult, op1=ALU.subtract)
        S.chained("vector", "VE", "tensor_scalar", out=SCC, in0=SCC,
                  scalar1=negR[:, tt:tt + 1], scalar2=None, op0=ALU.mult)
        S.wait("scalar", "VE")
        S.call("scalar", "activation", out=SC16[:, tt * 32:(tt + 1) * 32],
               in_=SCC, func=AF.Sigmoid, inc=("AC", 1))
        ac_hist.append(S.cnt["AC"])
    for tt in range(NT):
        SC = SC16[:, tt * 32:(tt + 1) * 32]
        S.wait("vector", "AC", val=ac_hist[tt])
        S.chained("vector", "VE", "tensor_tensor", out=SCC, in0=SC, in1=gbr,
                  op=ALU.add)
        for g in range(4):
            sl = SCC[:, g * 8:(g + 1) * 8]
            S.chained("vector", "VE", "reduce_max", out=m1c, in_=sl, axis=AX.X)
            S.chained("vector", "VE", "tensor_scalar", out=T32[:, 0:8], in0=sl,
                      scalar1=m1c, scalar2=None, op0=ALU.is_equal)
            S.chained("vector", "VE", "scalar_tensor_tensor", out=T32[:, 0:8],
                      in0=T32[:, 0:8], scalar=-1e30, in1=sl,
                      op0=ALU.mult, op1=ALU.add)
            S.chained("vector", "VE", "reduce_max", out=m2c, in_=T32[:, 0:8], axis=AX.X)
            S.chained("vector", "VE", "tensor_tensor", out=gsum[:, g:g + 1],
                      in0=m1c, in1=m2c, op=ALU.add)
        S.chained("vector", "VE", "reduce_max", out=m1c, in_=gsum, axis=AX.X)
        S.chained("vector", "VE", "tensor_scalar", out=T32[:, 0:4], in0=gsum,
                  scalar1=m1c, scalar2=None, op0=ALU.is_equal)
        S.chained("vector", "VE", "scalar_tensor_tensor", out=T32[:, 0:4],
                  in0=T32[:, 0:4], scalar=-1e30, in1=gsum,
                  op0=ALU.mult, op1=ALU.add)
        S.chained("vector", "VE", "reduce_max", out=gthr, in_=T32[:, 0:4], axis=AX.X)
        for g in range(4):
            S.chained("vector", "VE", "tensor_scalar", out=gmask[:, g:g + 1],
                      in0=gsum[:, g:g + 1], scalar1=gthr, scalar2=None,
                      op0=ALU.is_ge)
            S.chained("vector", "VE", "tensor_scalar", out=MO[:, g * 8:(g + 1) * 8],
                      in0=SCC[:, g * 8:(g + 1) * 8], scalar1=gmask[:, g:g + 1],
                      scalar2=None, op0=ALU.mult)
        S.chained("vector", "VE", "tensor_copy", out=T32, in_=MO)
        for r in range(8):
            S.chained("vector", "VE", "reduce_max", out=thr8, in_=T32, axis=AX.X)
            if r < 7:
                S.chained("vector", "VE", "tensor_scalar", out=SCC, in0=T32,
                          scalar1=thr8, scalar2=None, op0=ALU.is_equal)
                S.chained("vector", "VE", "scalar_tensor_tensor", out=T32,
                          in0=SCC, scalar=-1e30, in1=T32,
                          op0=ALU.mult, op1=ALU.add)
        S.chained("vector", "VE", "tensor_scalar", out=T32, in0=MO,
                  scalar1=thr8, scalar2=None, op0=ALU.is_ge)
        S.chained("vector", "VE", "tensor_tensor", out=T32, in0=T32, in1=SC,
                  op=ALU.mult)
        S.chained("vector", "VE", "reduce_sum", out=den, in_=T32, axis=AX.X)
        S.chained("vector", "VE", "tensor_scalar", out=den, in0=den,
                  scalar1=1e-20, scalar2=None, op0=ALU.add)
        S.chained("vector", "VE", "reciprocal", out=den, in_=den)
        S.chained("vector", "VE", "tensor_scalar",
                  out=b.SMALL[:, SM_WFIN + tt * 32:SM_WFIN + (tt + 1) * 32],
                  in0=T32, scalar1=den, scalar2=2.5, op0=ALU.mult, op1=ALU.mult)


def emit_moe1_shared(S, b, dram):
    slot0 = b.WS[:, 0:8192]
    slot1 = b.WS[:, 8192:16384]
    gs = b.BUF_A[:, 0:2048]
    PTf = b.PT[:].bitcast(F32)

    def load(nm, slot):
        S.wait("gpsimd", "PE")
        S.wait("gpsimd", "VE")
        S.dma("gpsimd", slot.rearrange("p (c f) -> p c f", c=NC_),
              dram["m1_" + nm].rearrange("(c p) f -> p c f", p=128), "WD")
        S.wait("tensor", "WD")

    load("sg", slot0)
    load("su", slot1)
    for sl8 in range(8):
        t0 = sl8 * 256
        for ic in range(NC_):
            S.mm_group([(b.PA[:, ic * 256:(ic + 1) * 256],
                         slot0[:, cin * 1024 + ic * 128: cin * 1024 + (ic + 1) * 128],
                         b.XFM[:, cin * SEQ + t0: cin * SEQ + t0 + 256])
                        for cin in range(NC_)], ("PE", 1))
        for ic in range(NC_):
            upsum = (b.PB[:, ic * 256:(ic + 1) * 256] if ic < 4 else
                     (b.PC[:, (ic - 4) * 256:(ic - 3) * 256] if ic < 6 else
                      PTf[:, (ic - 6) * 256:(ic - 5) * 256]))
            S.mm_group([(upsum,
                         slot1[:, cin * 1024 + ic * 128: cin * 1024 + (ic + 1) * 128],
                         b.XFM[:, cin * SEQ + t0: cin * SEQ + t0 + 256])
                        for cin in range(NC_)], ("PE", 1))
        S.wait("scalar", "PE")
        S.call("scalar", "activation", out=gs, in_=b.PA[:], func=AF.Silu,
               inc=("AC", 1))
        S.wait("vector", "AC")
        S.wait("vector", "PE")
        for ic in range(NC_):
            upsum = (b.PB[:, ic * 256:(ic + 1) * 256] if ic < 4 else
                     (b.PC[:, (ic - 4) * 256:(ic - 3) * 256] if ic < 6 else
                      PTf[:, (ic - 6) * 256:(ic - 5) * 256]))
            S.call("vector", "tensor_tensor",
                   out=b.BUF_B[:, ic * SEQ + t0: ic * SEQ + t0 + 256],
                   in0=gs[:, ic * 256:(ic + 1) * 256], in1=upsum, op=ALU.mult,
                   inc=(("VE", 1) if ic == NC_ - 1 else None))
        S.wait("tensor", "VE")
    load("sd", slot0)
    S.wait("tensor", "AC")
    S.wait("tensor", "VE")
    for tt in range(NT):
        for hf in range(2):
            S.mm_group([(b.PA[:, hf * 512:(hf + 1) * 512],
                         b.BUF_B[:, ic * SEQ + tt * 128: ic * SEQ + (tt + 1) * 128],
                         slot0[:, ic * 1024 + hf * 512: ic * 1024 + (hf + 1) * 512])
                        for ic in range(NC_)], ("PE", 1))
        S.wait("vector", "PE")
        S.call("vector", "tensor_tensor", out=b.XRES[:, tt * D:(tt + 1) * D],
               in0=b.XRES[:, tt * D:(tt + 1) * D], in1=b.PA[:, 0:1024],
               op=ALU.add, inc=("VE", 1))
        S.wait("tensor", "VE")


class Ctr:

    _cache = {}

    def __init__(self, nc, eng, name, init):
        key = (id(nc), eng.engine, name)
        if key in Ctr._cache:
            self.reg = Ctr._cache[key]
        else:
            regs = nc.alloc_registers(name, engines=[eng.engine])
            self.reg = list(regs)[0]
            Ctr._cache[key] = self.reg
        self.eng = eng
        nc.reg_mov(self.reg, init)

    def bump_wait(self, sem, delta):
        if delta:
            self.eng.scalar_reg_alu(ALU.add, self.reg, delta)
        self.eng.wait_ge(sem, self.reg)


def emit_moe1_experts(S, b, dram):
    wg = [b.WS[:, 0:4096], b.WS[:, 8192:8192 + 4096]]
    wu = [b.WS[:, 4096:8192], b.WS[:, 8192 + 4096:16384]]
    wd = [b.BUF_A[:, 0:4096], b.BUF_A[:, 4096:8192]]
    afm = [b.BUF_B[:, 0:2048], b.BUF_B[:, 2048:4096]]
    gsx = b.BUF_B[:, 4096:6144]
    PTf = b.PT[:].bitcast(F32)
    sems = S.sems
    dm = {"g": dram["m1_eg"], "u": dram["m1_eu"], "d": dram["m1_ed"]}
    NSLAB = 4
    wfv = b.SMALL[:].rearrange("p (t e) -> p t e", e=32)
    WOFF = SM_WFIN // 32
    wcur = [b.SMALL[:, SM_X + 32:SM_X + 48], b.SMALL[:, SM_X + 48:SM_X + 64]]

    wd0 = S.cnt["WD"]
    S.wait("gpsimd", "PE")
    S.wait("gpsimd", "VE")
    for p in range(2):
        S.dma("gpsimd", wg[p].rearrange("p (c i) -> p c i", c=NC_),
              dm["g"][p].rearrange("(c p) i -> p c i", p=128), "WD")
        S.dma("gpsimd", wu[p].rearrange("p (c i) -> p c i", c=NC_),
              dm["u"][p].rearrange("(c p) i -> p c i", p=128), "WD")
        S.dma("gpsimd", wd[p].rearrange("p (c h) -> p c h", c=4),
              dm["d"][p].rearrange("(c p) h -> p c h", p=128), "WD")

    pe0 = S.cnt["PE"]
    ve0 = S.cnt["VE"]
    ac0 = S.cnt["AC"]
    fz0 = S.cnt["FZ"]
    NSLAB = 8
    gsx2 = [b.BUF_B[:, 4096:5120], b.BUF_B[:, 5120:6144]]
    afm2 = [b.BUF_B[:, 0:1024], b.BUF_B[:, 1024:2048]]

    def g_psum(h):
        return b.PA[:, h * 1024:(h + 1) * 1024]

    def u_piece(h, ic):
        if h == 0:
            return b.PB[:, ic * 256:(ic + 1) * 256]
        return (b.PC[:, ic * 256:(ic + 1) * 256] if ic < 2
                else PTf[:, (ic - 2) * 256:(ic - 1) * 256])

    def d_out(h, tt_local, half):
        if tt_local == 0:
            return b.PA[:, h * 1024 + half * 512: h * 1024 + (half + 1) * 512]
        if h == 0:
            return b.PB[:, half * 512:(half + 1) * 512]
        return b.PC[:, 0:512] if half == 0 else PTf[:, 0:512]

    def GU(k):
        return 1 if k == 0 else (2 if k == 1 else 3 * k - 1)

    def D0(k):
        return 3 * k + 3 if k < NSLAB - 1 else 3 * NSLAB - 1
    def D1(k):
        return 3 * k + 4 if k < NSLAB - 1 else 3 * NSLAB

    S.begin_loop(16)
    for p in range(2):
        S.wait("gpsimd", "PE", val=pe0 + 24 * (p + 1))
        for nm, slot, c in (("g", wg, NC_), ("u", wu, NC_), ("d", wd, 4)):
            def fn(eng, nm=nm, slot=slot, p=p, c=c):
                i = S.loop_vars["gpsimd"][-1]
                e = i * 2 + 2 + p
                eng.dma_start(out=slot[p].rearrange("p (c i) -> p c i", c=c),
                              in_=dm[nm][e].rearrange("(c p) i -> p c i", p=128)
                              ).then_inc(sems["WD"], 16)
            S.raw("gpsimd", fn)
            S.inc("WD", 16)
    for p in range(2):
        e_first = None
        def gfn(eng, p=p):
            i = S.loop_vars["vector"][-1]
            e = i * 2 + p
            eng.tensor_copy(out=wcur[p], in_=wfv[:, WOFF:WOFF + 16, e]
                            ).then_inc(sems["FZ"], 1)
        S.raw("vector", gfn)
        S.inc("FZ", 1)
        S.wait("vector", "FZ", val=fz0 + p + 1)
        for k in range(NSLAB):
            n = p * NSLAB + k
            h = k % 2
            S.wait("vector", "AC", val=ac0 + n + 1)
            S.wait("vector", "PE", val=pe0 + 24 * p + GU(k))
            if h == 0:
                S.call("vector", "tensor_tensor", out=afm2[0],
                       in0=gsx2[0], in1=b.PB[:, 0:1024], op=ALU.mult,
                       inc=("VE", 1))
            else:
                S.call("vector", "tensor_tensor", out=afm2[1][:, 0:512],
                       in0=gsx2[1][:, 0:512], in1=b.PC[:, 0:512], op=ALU.mult)
                S.call("vector", "tensor_tensor", out=afm2[1][:, 512:1024],
                       in0=gsx2[1][:, 512:1024], in1=PTf[:, 0:512], op=ALU.mult,
                       inc=("VE", 1))
            for tl in range(2):
                tile = 2 * k + tl
                wc = wcur[p][:, tile:tile + 1]
                xr = b.XRES[:, tile * D:(tile + 1) * D]
                S.wait("vector", "PE",
                       val=pe0 + 24 * p + (D0(k) if tl == 0 else D1(k)))
                if tl == 0:
                    S.call("vector", "scalar_tensor_tensor", out=xr,
                           in0=g_psum(h), scalar=wc, in1=xr,
                           op0=ALU.mult, op1=ALU.add, inc=("VE", 1))
                elif h == 0:
                    S.call("vector", "scalar_tensor_tensor", out=xr,
                           in0=b.PB[:, 0:1024], scalar=wc, in1=xr,
                           op0=ALU.mult, op1=ALU.add, inc=("VE", 1))
                else:
                    S.call("vector", "scalar_tensor_tensor", out=xr[:, 0:512],
                           in0=b.PC[:, 0:512], scalar=wc, in1=xr[:, 0:512],
                           op0=ALU.mult, op1=ALU.add)
                    S.call("vector", "scalar_tensor_tensor", out=xr[:, 512:1024],
                           in0=PTf[:, 0:512], scalar=wc, in1=xr[:, 512:1024],
                           op0=ALU.mult, op1=ALU.add, inc=("VE", 1))
    for p in range(2):
        S.wait("tensor", "WD", val=wd0 + 48 * (p + 1))

        def emit_gu(k):
            t0 = k * 256
            n = p * NSLAB + k
            h = k % 2
            if n == 0:
                S.wait("tensor", "VE", val=ve0)
            if n >= 2:
                S.wait("tensor", "VE", val=ve0 + 3 * (n - 2) + 3)
            groups = []
            for ic in range(4):
                groups.append([(g_psum(h)[:, ic * 256:(ic + 1) * 256],
                                wg[p][:, cin * 512 + ic * 128:
                                      cin * 512 + (ic + 1) * 128],
                                b.XFM[:, cin * SEQ + t0: cin * SEQ + t0 + 256])
                               for cin in range(NC_)])
            for ic in range(4):
                groups.append([(u_piece(h, ic),
                                wu[p][:, cin * 512 + ic * 128:
                                      cin * 512 + (ic + 1) * 128],
                                b.XFM[:, cin * SEQ + t0: cin * SEQ + t0 + 256])
                               for cin in range(NC_)])
            S.mm_multi(groups, ("PE", 1))

        def emit_down(k):
            n = p * NSLAB + k
            h = k % 2
            S.wait("tensor", "VE", val=ve0 + 3 * n + 1)
            S.wait("tensor", "AC", val=ac0 + n + 1)
            for tl in range(2):
                groups = []
                for half in range(2):
                    groups.append([(d_out(h, tl, half),
                                    afm2[h][:, ic * 256 + tl * 128:
                                            ic * 256 + tl * 128 + 128],
                                    wd[p][:, ic * 1024 + half * 512:
                                          ic * 1024 + (half + 1) * 512])
                                   for ic in range(4)])
                S.mm_multi(groups, ("PE", 1))

        emit_gu(0)
        for k in range(1, NSLAB):
            emit_gu(k)
            emit_down(k - 1)
        emit_down(NSLAB - 1)
    for p in range(2):
        for k in range(NSLAB):
            n = p * NSLAB + k
            h = k % 2
            S.wait("scalar", "PE", val=pe0 + 24 * p + GU(k))
            if 3 * n - 5 > 0:
                S.wait("scalar", "VE", val=max(ve0 + 3 * n - 5, 1))
            S.call("scalar", "activation", out=gsx2[h], in_=g_psum(h),
                   func=AF.Silu, inc=("AC", 1))
    S.end_loop()




def _prep_consts():
    q = np.arange(128)
    return {
        "ones128": np.ones((128, 1), np.float32),
        "ident": np.eye(128, dtype=np.float32),
        "ones512r": np.ones((1, 512), np.float32),
        "tri_lo": np.where(q[:, None] >= q[None, :], 0.0, -1e30).astype(np.float32),
        "tri_hi": np.where(q[:, None] <= q[None, :], 0.0, -1e30).astype(np.float32),
    }


CONST_SHAPES = [("ones128", [128, 1]), ("ident", [128, 128]),
                ("ones512r", [1, 512]), ("tri_lo", [128, 128]),
                ("tri_hi", [128, 128])]


def _prep_ffn0(inp):
    sel = np.zeros((8, 512), np.float32)
    for cc in range(4):
        for p in range(128):
            sel[cc * 2 + p // 64, cc * 128 + p] = 1.0
    return {
        "f0_gw_flat": np.ascontiguousarray(
            inp["L0_ffn_gate_w"].reshape(1, 8192)),
        "f0_WdT": np.ascontiguousarray(inp["L0_ffn_down_w"].reshape(512, 1024).T),
        "f0_bd": np.ascontiguousarray(inp["L0_ffn_down_b"].reshape(512)),
        "f0_WuT": np.ascontiguousarray(
            inp["L0_ffn_up_w"].transpose(0, 2, 1).reshape(512, 1024)),
        "f0_bu": np.ascontiguousarray(inp["L0_ffn_up_b"]),
        "f0_SEL": sel,
    }


FFN0_SHAPES = [("f0_gw_flat", [1, 8192]), ("f0_WdT", [1024, 512]),
               ("f0_bd", [512]), ("f0_WuT", [512, 1024]),
               ("f0_bu", [8, 1024]), ("f0_SEL", [8, 512])]


def _prep_attn_b0(inp):
    W = np.asarray(inp["L0_attn_b_W_qkv"])
    bqkv = np.asarray(inp["L0_attn_b_b_qkv"])
    cols = np.arange(3072)
    h = cols // 192
    j = cols % 192
    qcols = (h * 64 + j)[j < 64]
    kcols = (h * 64 + (j - 64))[(j >= 64) & (j < 128)]
    vcols = (h * 64 + (j - 128))[j >= 128]
    qi = np.zeros(1024, np.int64); qi[qcols] = cols[j < 64]
    ki = np.zeros(1024, np.int64); ki[kcols] = cols[(j >= 64) & (j < 128)]
    vi = np.zeros(1024, np.int64); vi[vcols] = cols[j >= 128]
    return {
        "ab0_Wq": np.ascontiguousarray(W[:, qi]),
        "ab0_Wk": np.ascontiguousarray(W[:, ki]),
        "ab0_Wv": np.ascontiguousarray(W[:, vi]),
        "ab0_Wo": np.ascontiguousarray(inp["L0_attn_b_W_out"]),
        "ab0_bq": np.ascontiguousarray(bqkv[qi]),
        "ab0_bk": np.ascontiguousarray(bqkv[ki]),
        "ab0_bv": np.ascontiguousarray(bqkv[vi]),
        "ab0_bo": np.ascontiguousarray(inp["L0_attn_b_b_out"]),
    }


ATTN_B0_SHAPES = [("ab0_" + n, [1024, 1024]) for n in ("Wq", "Wk", "Wv", "Wo")] + \
                 [("ab0_" + n, [1024]) for n in ("bq", "bk", "bv", "bo")]




def build_program(sublayers):
    nc = bass.Bass("TRN2", target_bir_lowering=False)
    dram = {}
    dram["x_in"] = nc.dram_tensor("x_in", [SPC, SEQ, D], F32,
                                  kind="ExternalInput").ap()
    dram["x_out"] = nc.dram_tensor("x_out", [SPC, SEQ, D], F32,
                                   kind="ExternalOutput").ap()
    shapes = list(CONST_SHAPES)
    if "ffn0" in sublayers:
        shapes += FFN0_SHAPES
    if "attn_b0" in sublayers:
        shapes += ATTN_B0_SHAPES
    if "attn_a0" in sublayers:
        shapes += ATTN_A0_SHAPES
    if "attn_a1" in sublayers:
        shapes += ATTN_A1_SHAPES
    if "attn_b1" in sublayers:
        shapes += ATTN_B1_SHAPES
    if "moe1" in sublayers:
        shapes += MOE1_SHAPES
    for name, shape in shapes:
        dram[name] = nc.dram_tensor(name, shape, F32, kind="ExternalInput").ap()

    with ExitStack() as ctx:
        b = alloc_bufs(nc, ctx)
        sems = {}
        for name in ("WD", "XD", "PE", "VE", "AC", "FZ"):
            sems[name] = ctx.enter_context(nc.semaphore(name))
        S = Sched(sems, nc)
        emit_consts(S, b, dram)
        for e in ("vector", "tensor", "scalar"):
            S.wait(e, "WD")
            S.wait(e, "XD")
        S.begin_loop(SPC)
        if True:
            S.wait("sync", "XD")
            S.wait("sync", "VE")

            def ldx(eng):
                s = S.loop_vars["sync"][-1]
                eng.dma_start(out=b.XRES[:].rearrange("p (i d) -> p i d", i=NT),
                              in_=dram["x_in"][s].rearrange("(i p) d -> p i d", p=128)
                              ).then_inc(sems["XD"], 16)
            S.raw("sync", ldx)
            S.inc("XD", 16)
            S.wait("vector", "XD")
            S.wait("scalar", "XD")
            S.wait("gpsimd", "XD")
            S.wait("tensor", "XD")
            for kind in sublayers:
                if kind == "ffn0":
                    emit_ln0(S, b)
                    emit_ffn0(S, b, dram)
                elif kind == "attn_a0":
                    S.dma("sync", b.CONST[:, C_G:C_G + 8], dram["aa0_g8"], "XD")
                    S.wait("vector", "XD")
                    emit_ln0(S, b, scale_cols=b.SMALL[:, SMX_S:SMX_S + NT],
                             post_stats=lambda: emit_aa0_cols(S, b))
                    emit_attn_a0(S, b, dram)
                elif kind == "attn_b0":
                    emit_ln0(S, b)
                    emit_mha(S, b, dram, "ab0")
                elif kind == "attn_a1":
                    S.dma("sync", b.CONST[:, C_G:C_G + 8], dram["aa1_g8"], "XD")
                    S.dma("gpsimd", b.CBF[:, CB_B8:CB_B8 + 8], dram["aa1_b8"], "WD")
                    S.wait("vector", "XD")
                    S.wait("tensor", "XD")
                    S.wait("tensor", "WD")
                    emit_ln0(S, b, scale_cols=b.SMALL[:, SMX_S:SMX_S + NT],
                             post_stats=lambda: emit_ln2cols(S, b, want_resid=True))
                    emit_mha2(S, b, dram, "aa1", rms=False, band=True,
                              inner_beta=True, out_bias=False, inner_residual=True)
                elif kind == "attn_b1":
                    S.dma("sync", b.CONST[:, C_G:C_G + 8], dram["ab1_g8"], "XD")
                    S.dma("gpsimd", b.CBF[:, CB_B8:CB_B8 + 8], dram["ab1_b8"], "WD")
                    S.dma("sync", b.CONST[:, C_QG:C_QG + 1], dram["ab1_qg"], "XD")
                    S.dma("sync", b.CONST[:, C_KG:C_KG + 1], dram["ab1_kg"], "XD")
                    S.wait("vector", "XD")
                    S.wait("tensor", "XD")
                    S.wait("tensor", "WD")
                    emit_ln0(S, b, scale_cols=b.SMALL[:, SMX_S:SMX_S + NT],
                             post_stats=lambda: emit_ln2cols(S, b, want_resid=False))
                    emit_mha2(S, b, dram, "ab1", rms=True, band=False,
                              inner_beta=True, out_bias=True, inner_residual=False)
                elif kind == "moe1":
                    emit_ln0(S, b)
                    emit_moe1_gate(S, b, dram)
                    emit_moe1_shared(S, b, dram)
                    emit_moe1_experts(S, b, dram)
                else:
                    raise ValueError(kind)
            S.wait("sync", "VE")

            def stx(eng):
                s = S.loop_vars["sync"][-1]
                eng.dma_start(out=dram["x_out"][s].rearrange("(i p) d -> p i d", p=128),
                              in_=b.XRES[:].rearrange("p (i d) -> p i d", i=NT)
                              ).then_inc(sems["XD"], 16)
            S.raw("sync", stx)
            S.inc("XD", 16)
        S.end_loop()
        S.wait("sync", "XD")
        emit_block(nc, S)
    return nc


def _prep_attn_a1(inp):
    T = np.ascontiguousarray(np.asarray(inp["L1_attn_a_w_qkv"]).T)
    return {
        "aa1_Wq": np.ascontiguousarray(T[:, 0:1024]),
        "aa1_Wk": np.ascontiguousarray(T[:, 1024:2048]),
        "aa1_Wv": np.ascontiguousarray(T[:, 2048:3072]),
        "aa1_Wo": np.ascontiguousarray(np.asarray(inp["L1_attn_a_w_out"]).T),
        "aa1_g8": np.ascontiguousarray(np.asarray(inp["L1_attn_a_gamma"]).reshape(8, 128).T),
        "aa1_b8": np.ascontiguousarray(np.asarray(inp["L1_attn_a_beta"]).reshape(8, 128).T),
    }


ATTN_A1_SHAPES = [("aa1_" + n, [1024, 1024]) for n in ("Wq", "Wk", "Wv", "Wo")] + \
                 [("aa1_g8", [128, 8]), ("aa1_b8", [128, 8])]


def _prep_attn_b1(inp):
    return {
        "ab1_Wq": np.ascontiguousarray(np.asarray(inp["L1_attn_b_Wq"]).T),
        "ab1_Wk": np.ascontiguousarray(np.asarray(inp["L1_attn_b_Wk"]).T),
        "ab1_Wv": np.ascontiguousarray(np.asarray(inp["L1_attn_b_Wv"]).T),
        "ab1_Wo": np.ascontiguousarray(np.asarray(inp["L1_attn_b_Wo"]).T),
        "ab1_bo": np.ascontiguousarray(np.asarray(inp["L1_attn_b_bo"])),
        "ab1_g8": np.ascontiguousarray(np.asarray(inp["L1_attn_b_ln_g"]).reshape(8, 128).T),
        "ab1_b8": np.ascontiguousarray(np.asarray(inp["L1_attn_b_ln_b"]).reshape(8, 128).T),
        "ab1_qg": np.ascontiguousarray(np.tile(np.asarray(inp["L1_attn_b_q_gamma"]), 2)[:, None]),
        "ab1_kg": np.ascontiguousarray(np.tile(np.asarray(inp["L1_attn_b_k_gamma"]), 2)[:, None]),
    }


ATTN_B1_SHAPES = [("ab1_" + n, [1024, 1024]) for n in ("Wq", "Wk", "Wv", "Wo")] + \
                 [("ab1_bo", [1024]), ("ab1_g8", [128, 8]), ("ab1_b8", [128, 8]),
                  ("ab1_qg", [128, 1]), ("ab1_kg", [128, 1])]


def _prep_attn_a0(inp):
    W = np.asarray(inp["L0_attn_a_w_qkv"])
    return {
        "aa0_Wq": np.ascontiguousarray(W[:, 0:1024]),
        "aa0_Wk": np.ascontiguousarray(W[:, 1024:2048]),
        "aa0_Wv": np.ascontiguousarray(W[:, 2048:3072]),
        "aa0_Wo": np.ascontiguousarray(np.asarray(inp["L0_attn_a_w_out"])),
        "aa0_g8": np.ascontiguousarray(np.asarray(inp["L0_attn_a_gamma"]).reshape(8, 128).T),
        "aa0_temp": np.ascontiguousarray(np.asarray(inp["L0_attn_a_temperature"]).reshape(1, 16)),
    }


ATTN_A0_SHAPES = [("aa0_" + n, [1024, 1024]) for n in ("Wq", "Wk", "Wv", "Wo")] + \
                 [("aa0_g8", [128, 8]), ("aa0_temp", [1, 16])]


def _pad34(a):
    return np.ascontiguousarray(np.concatenate([a, a[:2]], axis=0))


def _prep_moe1(inp):
    return {
        "m1_gwT": np.ascontiguousarray(np.asarray(inp["L1_ffn_gate_w"]).T),
        "m1_gbias": np.ascontiguousarray(np.asarray(inp["L1_ffn_gate_bias"])),
        "m1_eg": _pad34(np.asarray(inp["L1_ffn_expert_gate"]).transpose(0, 2, 1)),
        "m1_eu": _pad34(np.asarray(inp["L1_ffn_expert_up"]).transpose(0, 2, 1)),
        "m1_ed": _pad34(np.asarray(inp["L1_ffn_expert_down"]).transpose(0, 2, 1)),
        "m1_sg": np.ascontiguousarray(np.asarray(inp["L1_ffn_shared_gate"]).T),
        "m1_su": np.ascontiguousarray(np.asarray(inp["L1_ffn_shared_up"]).T),
        "m1_sd": np.ascontiguousarray(np.asarray(inp["L1_ffn_shared_down"]).T),
    }


MOE1_SHAPES = [("m1_gwT", [1024, 32]), ("m1_gbias", [32]),
               ("m1_eg", [34, 1024, 512]), ("m1_eu", [34, 1024, 512]),
               ("m1_ed", [34, 512, 1024]),
               ("m1_sg", [1024, 1024]), ("m1_su", [1024, 1024]),
               ("m1_sd", [1024, 1024])]


PREP = {"ffn0": _prep_ffn0, "attn_b0": _prep_attn_b0,
        "attn_a1": _prep_attn_a1, "attn_b1": _prep_attn_b1,
        "attn_a0": _prep_attn_a0, "moe1": _prep_moe1}
SUB_SHAPES = {"ffn0": FFN0_SHAPES}


CHAIN = ["attn_a0", "ffn0", "attn_b0", "ffn0", "attn_a1", "moe1", "attn_b1", "moe1"]

_PROG_CACHE = {}


def _get_program(kinds):
    key = tuple(kinds)
    if key not in _PROG_CACHE:
        _PROG_CACHE[key] = build_program(list(kinds))
    return _PROG_CACHE[key]


def kernel(**inputs):
    inputs = {k: np.asarray(v) for k, v in inputs.items()}
    x = np.ascontiguousarray(inputs["L0_attn_a_x"], dtype=np.float32)
    consts = _prep_consts()
    weights = {}
    for kind in set(CHAIN):
        weights[kind] = PREP[kind](inputs)

    nc = _get_program(CHAIN)
    in_maps = []
    for c in range(N_CORES):
        m = {"x_in": np.ascontiguousarray(x[c * SPC:(c + 1) * SPC])}
        m.update(consts)
        for kind in set(CHAIN):
            m.update(weights[kind])
        in_maps.append(m)
    res = run_bass_kernel_spmd(nc, in_maps, core_ids=list(range(N_CORES)))
    return np.concatenate([r["x_out"] for r in res.results], axis=0)
